# version 31
# baseline (speedup 1.0000x reference)
import sys
sys.path.insert(0, '/opt/trn_rl_repo')
import math
import numpy as np
import ml_dtypes

import concourse.bass as bass
import concourse.mybir as mybir
from concourse.bass_utils import run_bass_kernel_spmd

F32 = mybir.dt.float32
BF16 = mybir.dt.bfloat16
AF = mybir.ActivationFunctionType
ALU = mybir.AluOpType
AX = mybir.AxisListType
BF = ml_dtypes.bfloat16

NCORES = 8
T = 2048
H = 2048
NC = 16
EPS_LN = 1e-5
EPS_RMS = 1e-5
NCHUNK = 16

DEBUG = False
DBG_NAMES = []

DO_ATTN = True
DO_MOE1 = True
DO_MAMBA = True
DO_MOE2 = True


class Ctr:
    def __init__(self, h, name):
        self.h = h
        self.name = name
        self.n = 0


class Sched:
    ENGINES = ("tensor", "vector", "scalar", "sync", "gpsimd")

    def __init__(self):
        self.events = []
        self.all_ctrs = []
        self.own_ctr = {}

    def op(self, engine, fn, waits=(), inc=None, inc_k=1, no_own=False):
        wl = []
        for w in waits:
            if isinstance(w, tuple):
                c, v = w
            else:
                c, v = w, w.n
            assert v <= c.n, f"wait on future value of {c.name}: {v} > {c.n}"
            if v > 0:
                wl.append((c, v))
        own = None if no_own else self.own_ctr.get(engine)
        if own is not None and own.n > 0:
            wl.append((own, own.n))
        incs = []
        if inc is not None:
            incs.append((inc, inc_k))
            inc.n += inc_k
        if own is not None and not any(c is own for c, _ in incs):
            incs.append((own, 1))
            own.n += 1
        self.events.append((engine, fn, wl, incs))

    def join(self, engines=None):
        waits = [(c, c.n) for c in self.all_ctrs if c.n > 0]
        for e in (engines or self.ENGINES):
            self.events.append((e, None, list(waits), []))

    def emit(self, block):
        for eng_name in self.ENGINES:
            def body(eng, eng_name=eng_name):
                lastw = {}
                for engine, fn, waits, incs in self.events:
                    if engine != eng_name:
                        continue
                    for c, v in waits:
                        if lastw.get(c.name, -1) >= v:
                            continue
                        eng.wait_ge(c.h, v)
                        lastw[c.name] = v
                    if fn is None:
                        continue
                    inst = fn(eng)
                    for c, k in incs:
                        assert inst is not None
                        inst.then_inc(c.h, k)
            getattr(block, eng_name)(body)


def build_nc():
    nc = bass.Bass("TRN2", target_bir_lowering=False, debug=False)

    def din(name, shape):
        return nc.dram_tensor(name, list(shape), F32, kind="ExternalInput").ap()

    x_in = din("x_fm", [NC, 128, T])
    wq_d = din("wq_blk", [16, 128, 2048])
    wk_d = din("wk_blk", [16, 128, 2048])
    wv_d = din("wv_blk", [2, 16, 128, 1024])
    bq_d = din("bq_cols", [128, 16])
    bk_d = din("bk_cols", [128, 16])
    bv_d = din("bv_cols", [128, 16])
    wout_d = din("wout_blk", [16, 128, 2048])
    bout_d = din("bout_cols", [128, 16])
    wrtr_d = din("router_wT", [128, 128])
    w1_d = din("w1_blk", [8, 4, 128, 2048])
    w3_d = din("w3_blk", [8, 4, 128, 2048])
    w2_d = din("w2_blk", [8, 16, 128, 512])
    wg_d = din("wg_blk", [2, 16, 128, 2048])
    wxs_d = din("wxs_blk", [32, 128, 2048])
    wbc_d = din("wbc_blk", [128, 2048])
    wdt_d = din("wdt_blk", [16, 128, 64])
    cw_d = din("convw_cols", [128, 132])
    cb_d = din("convb_cols", [128, 33])
    dtb_d = din("dtbias_col", [64, 1])
    alog_d = din("alog_col", [64, 1])
    drow_d = din("d_row", [1, 64])
    nw_d = din("nw_row", [1, 4096])
    lnw_d = din("lnw_cols", [128, 16])
    wop_d = din("wop_blk", [16, 128, 4096])

    out_d = nc.dram_tensor("out_fm", [NC, 128, T], F32, kind="ExternalOutput").ap()
    global DBG_NAMES
    DBG_NAMES = []
    dbg_d = (nc.dram_tensor("dbg", [32, 128, 2048], BF16, kind="ExternalOutput").ap()
             if DEBUG else None)
    dbgf_d = (nc.dram_tensor("dbgf", [6, 128, 2048], F32, kind="ExternalOutput").ap()
              if DEBUG else None)

    def scr(name, shape, dt=BF16):
        return nc.dram_tensor(name, list(shape), dt, kind="Internal").ap()

    d_x = scr("scr_x", [NC, 128, T])
    d_v = scr("scr_v", [16, 128, 2048])
    d_pT = scr("scr_pT", [16, 128, 2048])
    d_gate = scr("scr_gate", [16, 4, 128, 1024])
    d_raw = scr("scr_raw", [33, 128, 2064])
    d_rows = scr("scr_rows", [8, 2048], F32)
    ROW_MUCUM, ROW_R, ROW_RHO3, ROW_R1P, ROW_RSC = 0, 1, 2, 3, 4
    d_flat = scr("scr_flat", [NCHUNK, 64, 128], F32)
    d_gfm = scr("scr_gfm", [8, 2048], F32)
    d_om = scr("scr_om", [16, 128, 2048])

    ones_col_d = nc.inline_tensor(np.ones((128, 1), np.float32), name="c_onescol").ap()
    ones_colb_d = nc.inline_tensor(np.ones((128, 1), dtype=BF), name="c_onescolb").ap()
    ones_row_d = nc.inline_tensor(np.ones((1, 128), np.float32), name="c_onesrow").ap()
    ident_bf_d = nc.inline_tensor(np.eye(128, dtype=BF), name="c_identbf").ap()
    ident_f_d = nc.inline_tensor(np.eye(128, dtype=np.float32), name="c_identf").ap()
    mask_np = (np.arange(128)[None, :] >= np.arange(128)[:, None]).astype(BF)
    mask_bf_d = nc.inline_tensor(mask_np, name="c_mask").ap()
    ones64_d = nc.inline_tensor(np.ones((1, 64), np.float32), name="c_ones64").ap()

    BASE = ((nc.sbuf_base + 31) // 32) * 32
    TOP = nc.sbuf_top

    def nbytes_of(shape, dt):
        per = int(np.prod(shape[1:])) if len(shape) > 1 else 1
        return per * (2 if dt == BF16 else 4)

    cursor = [BASE]

    def place(name, shape, dt):
        at = cursor[0]
        tns = nc.alloc_sbuf_tensor_at(name, list(shape), dt, offset=at)
        cursor[0] = at + ((nbytes_of(shape, dt) + 31) // 32) * 32
        assert cursor[0] <= TOP, f"sbuf overflow placing {name}"
        return tns

    X = [place(f"xt{c}", [128, T], BF16) for c in range(NC)]
    B1_OFF = cursor[0]
    place("big1", [128, 32768], BF16)
    B2_OFF = cursor[0]
    place("big2", [128, 32768], BF16)
    WBUF = [place(f"wbuf{i}", [128, 2048], BF16) for i in range(2)]
    OBUF = [place(f"obuf{i}", [128, 1024], BF16) for i in range(2)]
    onesc = place("onesc", [128, 1], F32)
    onesb2 = place("onesb2", [128, 1], BF16)
    onesr = place("onesr", [1, 128], F32)
    ones64 = place("ones64", [1, 64], F32)
    identb = place("identb", [128, 128], BF16)
    identf = place("identf", [128, 128], F32)
    maskb = place("maskb", [128, 128], BF16)
    bq_c = place("bq_c", [128, 16], F32)
    bk_c = place("bk_c", [128, 16], F32)
    bv_c = place("bv_c", [128, 16], F32)
    bout_c = place("bout_c", [128, 16], F32)
    lnw_c = place("lnw_c", [128, 16], F32)
    ilnw_c = place("ilnw_c", [128, 16], F32)
    rT_t = place("rT_t", [128, 16], F32)
    rscT_t = place("rscT_t", [128, 16], F32)
    rhoT_t = place("rhoT_t", [128, 16], F32)
    rtr_t = place("rtr_t", [128, 128], BF16)
    cw_t = place("cw_t", [128, 132], F32)
    cb_t = place("cb_t", [128, 33], F32)
    dtb_t = place("dtb_t", [64, 1], F32)
    alog_t = place("alog_t", [64, 1], F32)
    a_t = place("a_t", [64, 1], F32)
    nmax_t = place("nmax_t", [128, 8], F32)
    X_OFF = BASE
    SBUF_SMALL_END = cursor[0]
    assert SBUF_SMALL_END <= TOP

    def carve(base_off, byte_off, name, shape, dt):
        assert byte_off + nbytes_of(shape, dt) <= 65536, f"carve overflow {name}"
        return nc.alloc_sbuf_tensor_at(name, list(shape), dt, offset=base_off + byte_off)

    PS_A = nc.alloc_psum_tensor("psA", [128, 2048], F32)
    PS_B = nc.alloc_psum_tensor("psB", [128, 2048], F32)
    PS_Abf = PS_A.bitcast(BF16)
    PS_Bbf = PS_B.bitcast(BF16)

    S = Sched()

    def mkctr(name):
        c = Ctr(nc.alloc_semaphore(name), name)
        S.all_ctrs.append(c)
        return c

    gW = [mkctr("gW0"), mkctr("gW1")]
    gM = mkctr("gM")
    io = [mkctr("io0"), mkctr("io1")]
    ioT = [mkctr("ioT0"), mkctr("ioT1")]
    ioS = mkctr("ioS")
    gA = mkctr("gA")
    gG = mkctr("gG")
    tv = mkctr("tv")
    vv = mkctr("vv")
    sv = mkctr("sv")
    pv = mkctr("pv")
    S.own_ctr = {"vector": vv, "scalar": sv}

    def g_load(dst_ap, src_ap, ctr=None, waits=()):
        c = ctr if ctr is not None else gM
        S.op("gpsimd", lambda g, d=dst_ap, s=src_ap: g.dma_start(out=d, in_=s),
             waits=waits, inc=c, inc_k=16)

    def io_load(dst_ap, src_ap, parity=0, waits=(), slow=False):
        def fn(g, d=dst_ap, s=src_ap, slow=slow):
            return g.dma_start(out=d, in_=s, allow_slow_non_contiguous=slow)
        S.op("sync", fn, waits=waits, inc=io[parity], inc_k=16)

    def io_store(dst_ap, src_ap, waits=(), parity=None):
        c = ioS if parity is None else ioT[parity]
        S.op("sync", lambda g, d=dst_ap, s=src_ap: g.dma_start(out=d, in_=s),
             waits=waits, inc=c, inc_k=16)

    def tap(name, src_ap, f32=False):
        if not DEBUG:
            return
        arr = dbgf_d if f32 else dbg_d
        idx = len([n for n in DBG_NAMES if n[1] == f32])
        DBG_NAMES.append((name, f32, idx))
        P, Fr = src_ap.shape[0], src_ap.shape[1]
        S.op("sync", lambda g, d=arr[idx][0:P, 0:Fr], s=src_ap: g.dma_start(out=d, in_=s),
             waits=[(tv, tv.n), (vv, vv.n), (sv, sv.n)], inc=ioS, inc_k=16)

    for dst, src in ((onesc, ones_col_d), (onesr, ones_row_d), (identb, ident_bf_d),
                     (identf, ident_f_d), (maskb, mask_bf_d), (bq_c, bq_d),
                     (bk_c, bk_d), (bv_c, bv_d), (bout_c, bout_d), (lnw_c, lnw_d),
                     (rtr_t, wrtr_d), (onesb2, ones_colb_d), (ones64, ones64_d),
                     (cw_t, cw_d), (cb_t, cb_d), (dtb_t, dtb_d),
                     (alog_t, alog_d)):
        g_load(dst[...], src[...])
    for c in range(NC):
        g_load(X[c][:, :], x_in[c])
    zrow = carve(B1_OFF, 0, "zrow", [1, 2048], F32)
    S.op("vector", lambda v: v.memset(zrow[:, :], 0.0), inc=vv)
    io_store(d_rows[ROW_MUCUM:ROW_MUCUM + 1, :], zrow[:, :], waits=[vv])
    S.join()

    def emit_stats_and_center(tag, extra_rows=()):
        x2 = [carve(B1_OFF, 8192 * i, f"x2{i}_{tag}", [128, 2048], F32) for i in range(2)]
        r_mu = carve(B1_OFF, 16384, f"mu_{tag}", [1, 2048], F32)
        r_r = carve(B1_OFF, 24576, f"r_{tag}", [1, 2048], F32)
        r_t = carve(B1_OFF, 32768, f"t_{tag}", [1, 2048], F32)
        TV0 = tv.n

        for c in range(NC):
            if c % 2 == 0:
                S.op("scalar",
                     lambda s, c=c: s.activation(out=x2[c % 2][:, :],
                                                 in_=X[c][:, :],
                                                 func=AF.Square),
                     waits=[(tv, tv.n)], inc=sv)
                sq_w = (sv, sv.n)
            else:
                S.op("vector",
                     lambda v, c=c: v.tensor_tensor(x2[c % 2][:, :], X[c][:, :],
                                                    X[c][:, :], op=ALU.mult),
                     waits=[(tv, tv.n)], inc=vv)
                sq_w = (vv, vv.n)
            for nb in range(4):
                S.op("tensor", lambda t, c=c, nb=nb: t.matmul(
                    PS_B[0:1, nb * 512:(nb + 1) * 512], onesb2[:, :],
                    X[c][:, nb * 512:(nb + 1) * 512],
                    start=(c == 0), stop=(c == NC - 1)))
            for nb in range(4):
                S.op("tensor", lambda t, c=c, nb=nb: t.matmul(
                    PS_A[0:1, nb * 512:(nb + 1) * 512], onesc[:, :],
                    x2[c % 2][:, nb * 512:(nb + 1) * 512],
                    start=(c == 0), stop=(c == NC - 1)),
                    waits=([sq_w] if nb == 0 else []),
                    inc=(tv if nb == 3 else None))

        S.op("vector", lambda v: v.tensor_scalar_mul(r_mu[:, :], PS_B[0:1, :], 1.0 / H),
             waits=[tv])
        S.op("vector", lambda v: v.tensor_scalar_mul(r_t[:, :], PS_A[0:1, :], 1.0 / H))
        S.op("vector", lambda v: v.tensor_tensor(r_r[:, :], r_mu[:, :], r_mu[:, :],
                                                 op=ALU.mult))
        S.op("vector", lambda v: v.tensor_sub(r_t[:, :], r_t[:, :], r_r[:, :]))
        S.op("vector", lambda v: v.tensor_scalar_add(r_t[:, :], r_t[:, :], EPS_LN),
             inc=vv)
        S.op("scalar", lambda s: s.activation(out=r_t[:, :], in_=r_t[:, :], func=AF.Sqrt),
             waits=[vv], inc=sv)
        S.op("vector", lambda v: v.reciprocal(r_r[:, :], r_t[:, :]), waits=[sv], inc=vv)
        io_store(d_rows[ROW_R:ROW_R + 1, :], r_r[:, :], waits=[vv])
        for t16 in range(16):
            S.op("tensor", lambda t, t16=t16: t.matmul(
                PS_B[:, t16:t16 + 1], r_r[0:1, t16 * 128:(t16 + 1) * 128],
                onesr[0:1, 0:1], start=True, stop=True),
                waits=([(vv, vv.n)] if t16 == 0 else []),
                inc=(tv if t16 == 15 else None))
        S.op("vector", lambda v: v.tensor_copy(rT_t[:, :], PS_B[:, 0:16]),
             waits=[tv], inc=vv)
        if "rsc" in extra_rows:
            S.op("vector", lambda v: v.tensor_scalar_mul(rscT_t[:, :], rT_t[:, :],
                                                         1.0 / math.sqrt(H)),
                 inc=vv)
        if "r1p" in extra_rows:
            S.op("vector", lambda v: v.tensor_scalar_add(r_t[:, :], r_r[:, :], 1.0),
                 waits=[ioS], inc=vv)
            io_store(d_rows[ROW_R1P:ROW_R1P + 1, :], r_t[:, :], waits=[vv])
        if "rho3" in extra_rows:
            S.op("vector", lambda v: v.tensor_tensor(r_t[:, :], r_r[:, :], r_r[:, :],
                                                     op=ALU.mult), waits=[ioS])
            S.op("vector", lambda v: v.tensor_scalar(r_t[:, :], r_t[:, :], -EPS_LN,
                                                     1.0 + EPS_RMS, op0=ALU.mult,
                                                     op1=ALU.add), inc=vv)
            S.op("scalar", lambda s: s.activation(out=r_t[:, :], in_=r_t[:, :],
                                                  func=AF.Sqrt), waits=[vv], inc=sv)
            S.op("vector", lambda v: v.reciprocal(r_t[:, :], r_t[:, :]), waits=[sv])
            S.op("vector", lambda v: v.tensor_tensor(r_t[:, :], r_t[:, :], r_r[:, :],
                                                     op=ALU.mult), inc=vv)
            io_store(d_rows[ROW_RHO3:ROW_RHO3 + 1, :], r_t[:, :], waits=[vv])
            S.op("vector", lambda v: v.tensor_tensor(rhoT_t[:, :], rT_t[:, :],
                                                     rT_t[:, :], op=ALU.mult))
            S.op("vector", lambda v: v.tensor_scalar(rhoT_t[:, :], rhoT_t[:, :],
                                                     -EPS_LN, 1.0 + EPS_RMS,
                                                     op0=ALU.mult, op1=ALU.add),
                 inc=vv)
            S.op("scalar", lambda s: s.activation(out=rhoT_t[:, :], in_=rhoT_t[:, :],
                                                  func=AF.Sqrt), waits=[vv], inc=sv)
            S.op("vector", lambda v: v.reciprocal(rhoT_t[:, :], rhoT_t[:, :]),
                 waits=[sv])
            S.op("vector", lambda v: v.tensor_tensor(rhoT_t[:, :], rhoT_t[:, :],
                                                     rT_t[:, :], op=ALU.mult),
                 inc=vv)
        r_mc = carve(B1_OFF, 40960, f"mc_{tag}", [1, 2048], F32)
        io_load(r_mc[:, :], d_rows[ROW_MUCUM:ROW_MUCUM + 1, :], parity=1, waits=[ioS])
        S.op("vector", lambda v: v.tensor_add(r_mc[:, :], r_mc[:, :], r_mu[:, :]),
             waits=[io[1]], inc=vv)
        io_store(d_rows[ROW_MUCUM:ROW_MUCUM + 1, :], r_mc[:, :], waits=[vv])
        for nb in range(4):
            S.op("tensor", lambda t, nb=nb: t.matmul(
                PS_A[:, nb * 512:(nb + 1) * 512], onesr[:, :],
                r_mu[0:1, nb * 512:(nb + 1) * 512], start=True, stop=True),
                waits=[vv], inc=(tv if nb == 3 else None))
        for c in range(NC):
            S.op("vector", lambda v, c=c: v.tensor_tensor(
                X[c][:, :], X[c][:, :], PS_A[:, :], op=ALU.subtract),
                 waits=[tv], inc=(vv if c == NC - 1 else None))
        S.join()

    def emit_row_bcast(row_sb, dst_f32, ps, scale_const=None, extra_waits=()):
        for nb in range(4):
            S.op("tensor", lambda t, nb=nb: t.matmul(
                ps[:, nb * 512:(nb + 1) * 512], onesr[:, :],
                row_sb[0:1, nb * 512:(nb + 1) * 512], start=True, stop=True),
                waits=list(extra_waits), inc=(tv if nb == 3 else None))
        if scale_const is None:
            S.op("vector", lambda v: v.tensor_copy(dst_f32[:, :], ps[:, :]),
                 waits=[tv], inc=vv)
        else:
            S.op("vector", lambda v: v.tensor_scalar_mul(dst_f32[:, :], ps[:, :],
                                                         scale_const),
                 waits=[tv], inc=vv)

    def emit_attn():
        emit_stats_and_center("a1", extra_rows=("rsc",))
        tap("xc0", X[0][:, :])
        tap("xc15", X[15][:, :])

        wvh = [carve(B1_OFF, 2048 * c, f"wvh{c}", [128, 1024], BF16) for c in range(NC)]
        for oh in range(2):
            for c in range(NC):
                g_load(wvh[c][:, :], wv_d[oh, c], ctr=gW[oh],
                       waits=[(tv, tv.n)])
            avm = [sv.n, sv.n]
            for tt in range(16):
                ps = PS_A if tt % 2 == 0 else PS_B
                for c in range(NC):
                    for nb in range(2):
                        S.op("tensor", lambda t, c=c, nb=nb, tt=tt, ps=ps: t.matmul(
                            ps[:, nb * 512:(nb + 1) * 512],
                            X[c][:, tt * 128:(tt + 1) * 128],
                            wvh[c][:, nb * 512:(nb + 1) * 512],
                            start=(c == 0), stop=(c == NC - 1)),
                            waits=[(gW[oh], gW[oh].n), (sv, avm[tt % 2])],
                            inc=(tv if (c == NC - 1 and nb == 1) else None))
                ob = OBUF[tt % 2]
                S.op("scalar", lambda s, ps=ps, ob=ob: s.activation(
                    out=ob[:, :], in_=ps[:, 0:1024], func=AF.Copy),
                    waits=[tv, (ioT[tt % 2], ioT[tt % 2].n)], inc=sv)
                io_store(d_v[tt][:, oh * 1024:(oh + 1) * 1024], ob[:, :],
                         waits=[sv], parity=tt % 2)
                avm[tt % 2] = sv.n
        S.join()

        qT = [carve(B1_OFF, 4096 * i, f"qT{i}", [128, 2048], BF16) for i in range(16)]
        kT = [carve(B2_OFF, 4096 * i, f"kT{i}", [128, 2048], BF16) for i in range(16)]
        marks = [tv.n, tv.n]
        jobs = [(wq_d, qT, bq_c, ot) for ot in range(16)] + \
               [(wk_d, kT, bk_c, ot) for ot in range(16)]
        qkm = [sv.n, sv.n]
        for j, (wsrc, dstT, bcol, ot) in enumerate(jobs):
            par = j % 2
            g_load(WBUF[par][:, :], wsrc[ot], ctr=gW[par], waits=[(tv, marks[par])])
            ps = PS_A if par == 0 else PS_B
            for c in range(NC):
                for nb in range(4):
                    S.op("tensor", lambda t, c=c, nb=nb, par=par, ps=ps: t.matmul(
                        ps[:, nb * 512:(nb + 1) * 512],
                        WBUF[par][:, c * 128:(c + 1) * 128],
                        X[c][:, nb * 512:(nb + 1) * 512],
                        start=(c == 0), stop=(c == NC - 1)),
                        waits=[(gW[par], gW[par].n), (sv, qkm[par])],
                        inc=(tv if (c == NC - 1 and nb == 3) else None))
            marks[par] = tv.n
            S.op("scalar", lambda s, ps=ps, d=dstT[ot], b=bcol, ot=ot: s.activation(
                out=d[:, :], in_=ps[:, :], func=AF.Identity, bias=b[:, ot:ot + 1]),
                waits=[tv], inc=sv)
            qkm[par] = sv.n
        S.join()
        tap("q0", qT[0][:, :])
        tap("k0", kT[0][:, :])

        for c in range(NC):
            io_store(d_x[c], X[c][:, :])
        XSPILL = ioS.n
        S.join()

        scores = carve(X_OFF, 0, "scoresB", [128, 2048], F32)
        probs = carve(X_OFF, 8192, "probsB", [128, 2048], BF16)
        probs2 = carve(X_OFF, 12288, "probs2B", [128, 2048], BF16)
        probsT = carve(X_OFF, 16384, "probsTB", [128, 2048], BF16)
        R_b = carve(X_OFF, 20480, "R_bB", [128, 2048], F32)
        rrow = carve(X_OFF, 28672, "rrowB", [1, 2048], F32)
        io_load(rrow[:, :], d_rows[ROW_R:ROW_R + 1, :], parity=0, waits=[(ioS, XSPILL)])
        emit_row_bcast(rrow, R_b, PS_A, extra_waits=[(io[0], io[0].n)])
        S.join()

        bmark = [vv.n, vv.n]

        def emit_scores_mms(qt):
            ps = PS_A if qt % 2 == 0 else PS_B
            for c in range(NC):
                for nb in range(4):
                    S.op("tensor", lambda t, c=c, nb=nb, qt=qt, ps=ps: t.matmul(
                        ps[:, nb * 512:(nb + 1) * 512],
                        qT[c][:, qt * 128:(qt + 1) * 128],
                        kT[c][:, nb * 512:(nb + 1) * 512],
                        start=(c == 0), stop=(c == NC - 1)),
                        waits=([(vv, bmark[qt % 2])]
                               if (c == 0 and nb == 0) else []),
                        inc=(tv if (c == NC - 1 and nb == 3) else None))

        emit_scores_mms(0)
        for qt in range(16):
            ps = PS_A if qt % 2 == 0 else PS_B
            psbf = PS_Abf if qt % 2 == 0 else PS_Bbf
            SCM = tv.n
            S.op("vector", lambda v, ps=ps, qt=qt: v.scalar_tensor_tensor(
                scores[:, :], ps[:, :], rscT_t[:, qt:qt + 1], R_b[:, :],
                op0=ALU.mult, op1=ALU.mult),
                waits=[(tv, SCM), (sv, sv.n)], inc=vv)
            if qt < 15:
                emit_scores_mms(qt + 1)
            if qt == 0:
                tap("scores0", scores[:, :], f32=True)
                if DEBUG:
                    S.join()
            S.op("vector", lambda v: v.reduce_max(nmax_t[:, 1:2], scores[:, :],
                                                  axis=AX.X, negate=True), inc=vv)
            S.op("scalar", lambda s: s.activation(
                out=probs[:, :], in_=scores[:, :], func=AF.Exp,
                bias=nmax_t[:, 1:2], accum_out=nmax_t[:, 2:3]), waits=[vv], inc=sv)
            S.op("vector", lambda v: v.reciprocal(nmax_t[:, 3:4], nmax_t[:, 2:3]),
                 waits=[sv], inc=vv)
            S.op("vector", lambda v: v.scalar_tensor_tensor(
                probs2[:, :], probs[:, :], nmax_t[:, 3:4], R_b[:, :],
                op0=ALU.mult, op1=ALU.mult), inc=vv)
            if qt == 0:
                tap("probs0", probs[:, :])
                tap("probs20", probs2[:, :])
                tap("nmax0", nmax_t[:, :], f32=True)
                if DEBUG:
                    S.join()
            for kc in range(16):
                S.op("tensor", lambda t, kc=kc, psbf=psbf: t.transpose(
                    psbf[:, kc * 128:(kc + 1) * 128],
                    probs2[:, kc * 128:(kc + 1) * 128], identb[:, :]),
                    waits=[(vv, vv.n)], inc=(tv if kc == 15 else None))
            S.op("vector", lambda v, psbf=psbf: v.tensor_copy(
                probsT[:, :], psbf[:, 0:2048]),
                waits=[tv, (ioT[0], ioT[0].n), (ioT[1], ioT[1].n)], inc=vv)
            bmark[qt % 2] = vv.n
            if qt == 0:
                tap("probsT0", probsT[:, :])
                if DEBUG:
                    S.join()
            io_store(d_pT[qt], probsT[:, :], waits=[vv], parity=qt % 2)
        S.join()

        vT = [carve(B2_OFF, 4096 * i, f"vT{i}", [128, 2048], BF16) for i in range(16)]
        for i in range(16):
            io_load(vT[i][:, :], d_v[i], parity=i % 2)
        ctx = [carve(B1_OFF, 4096 * i, f"ctx{i}", [128, 2048], BF16) for i in range(16)]
        pt4 = [carve(X_OFF, 0, "pt4_0", [128, 8192], BF16),
               carve(X_OFF, 16384, "pt4_1", [128, 8192], BF16)]
        VL = (io[0].n, io[1].n)
        cmark = [tv.n, tv.n]
        smarkC = [sv.n, sv.n]
        PTL = (io[0].n, io[1].n)
        for qg in range(4):
            gpar = qg % 2
            buf = pt4[gpar]
            b3 = buf[:, :].rearrange("p (k j n) -> p k j n", j=4, n=128)
            for j in range(4):
                io_load(b3[:, :, j, :], d_pT[qg * 4 + j], parity=j % 2,
                        waits=[(tv, cmark[gpar]), (sv, smarkC[1 - gpar]),
                               (io[0], PTL[0]), (io[1], PTL[1])])
            PTL = (io[0].n, io[1].n)
            for htg in range(4):
                ps = PS_A if htg % 2 == 0 else PS_B
                for htl in range(4):
                    ht = htg * 4 + htl
                    for kc in range(16):
                        S.op("tensor", lambda t, ht=ht, htl=htl, kc=kc, ps=ps,
                             buf=buf: t.matmul(
                            ps[:, htl * 512:(htl + 1) * 512],
                            vT[kc][:, ht * 128:(ht + 1) * 128],
                            buf[:, kc * 512:(kc + 1) * 512],
                            start=(kc == 0), stop=(kc == 15)),
                            waits=[(io[0], PTL[0]), (io[1], PTL[1]),
                                   (sv, sv.n)],
                            inc=(tv if kc == 15 else None))
                for htl in range(4):
                    ht = htg * 4 + htl
                    S.op("scalar", lambda s, ht=ht, htl=htl, qg=qg, ps=ps: s.activation(
                        out=ctx[ht][:, qg * 512:(qg + 1) * 512],
                        in_=ps[:, htl * 512:(htl + 1) * 512],
                        func=AF.Identity, bias=bv_c[:, ht:ht + 1]),
                        waits=[(tv, tv.n)], inc=sv)
            cmark[gpar] = tv.n
        S.join()
        tap("ctx0", ctx[0][:, :])
        tap("ctx15", ctx[15][:, :])

        marks = [tv.n, tv.n]
        vmarks = [vv.n, vv.n]
        for ot in range(16):
            par = ot % 2
            g_load(WBUF[par][:, :], wout_d[ot], ctr=gW[par], waits=[(tv, marks[par])])
            io_load(X[ot][:, :], d_x[ot], parity=par,
                    waits=[(tv, marks[par]), (vv, vmarks[par])])
            ps = PS_A if par == 0 else PS_B
            for hc in range(NC):
                for nb in range(4):
                    S.op("tensor", lambda t, hc=hc, nb=nb, par=par, ps=ps: t.matmul(
                        ps[:, nb * 512:(nb + 1) * 512],
                        WBUF[par][:, hc * 128:(hc + 1) * 128],
                        ctx[hc][:, nb * 512:(nb + 1) * 512],
                        start=(hc == 0), stop=(hc == NC - 1)),
                        waits=[(gW[par], gW[par].n), (vv, vmarks[par])],
                        inc=(tv if (hc == NC - 1 and nb == 3) else None))
            marks[par] = tv.n
            S.op("vector", lambda v, ot=ot, ps=ps, par=par: v.scalar_tensor_tensor(
                X[ot][:, :], ps[:, :], bout_c[:, ot:ot + 1], X[ot][:, :],
                op0=ALU.add, op1=ALU.add),
                waits=[tv, (io[par], io[par].n)], inc=vv)
            vmarks[par] = vv.n
        S.join()

    def emit_final():
        mcrow = carve(B1_OFF, 0, "mcrow_f", [1, 2048], F32)
        ofin = [carve(B1_OFF, 8192 + 8192 * i, f"ofin{i}", [128, 2048], F32)
                for i in range(2)]
        io_load(mcrow[:, :], d_rows[ROW_MUCUM:ROW_MUCUM + 1, :], parity=0,
                waits=[(ioS, ioS.n)])
        for nb in range(4):
            S.op("tensor", lambda t, nb=nb: t.matmul(
                PS_B[:, nb * 512:(nb + 1) * 512], onesr[:, :],
                mcrow[0:1, nb * 512:(nb + 1) * 512], start=True, stop=True),
                waits=[(io[0], io[0].n)], inc=(tv if nb == 3 else None))
        for c in range(NC):
            S.op("vector", lambda v, c=c: v.tensor_tensor(
                ofin[c % 2][:, :], X[c][:, :], PS_B[:, :], op=ALU.add),
                waits=[tv, (ioT[c % 2], ioT[c % 2].n)], inc=vv)
            io_store(out_d[c], ofin[c % 2][:, :], waits=[vv], parity=c % 2)
        S.op("sync", lambda g: g.nop(),
             waits=[(ioT[0], ioT[0].n), (ioT[1], ioT[1].n), (ioS, ioS.n)])

    moe_mamba = _make_moe_mamba(
        nc=nc, S=S, carve=carve, X=X, B1_OFF=B1_OFF, B2_OFF=B2_OFF, X_OFF=X_OFF,
        PS_A=PS_A, PS_B=PS_B, PS_Abf=PS_Abf, PS_Bbf=PS_Bbf,
        WBUF=WBUF, OBUF=OBUF, gW=gW, gM=gM, io=io, ioT=ioT, ioS=ioS, gA=gA, gG=gG,
        tv=tv, vv=vv, sv=sv, pv=pv, g_load=g_load, io_load=io_load, io_store=io_store,
        emit_stats_and_center=emit_stats_and_center, emit_row_bcast=emit_row_bcast,
        tap=tap,
        d=dict(d_x=d_x, d_v=d_v, d_pT=d_pT, d_gate=d_gate, d_raw=d_raw,
               d_rows=d_rows, d_flat=d_flat, d_gfm=d_gfm, d_om=d_om),
        w=dict(w1_d=w1_d, w3_d=w3_d, w2_d=w2_d, wg_d=wg_d, wxs_d=wxs_d,
               wbc_d=wbc_d, wdt_d=wdt_d, drow_d=drow_d, nw_d=nw_d, wop_d=wop_d),
        t=dict(onesc=onesc, onesr=onesr, ones64=ones64, identb=identb,
               identf=identf, maskb=maskb,
               lnw_c=lnw_c, ilnw_c=ilnw_c, rT_t=rT_t, rscT_t=rscT_t, rhoT_t=rhoT_t,
               rtr_t=rtr_t, cw_t=cw_t, cb_t=cb_t, dtb_t=dtb_t, alog_t=alog_t,
               a_t=a_t, nmax_t=nmax_t),
        rows=dict(ROW_MUCUM=ROW_MUCUM, ROW_R=ROW_R, ROW_RHO3=ROW_RHO3,
                  ROW_R1P=ROW_R1P, ROW_RSC=ROW_RSC))

    if DO_ATTN:
        emit_attn()
    if DO_MOE1:
        moe_mamba.emit_moe("m1")
    if DO_MAMBA:
        moe_mamba.emit_mamba()
    if DO_MOE2:
        moe_mamba.emit_moe("m2")
    emit_final()

    with nc.Block() as block:
        S.emit(block)
    return nc


class _make_moe_mamba:
    def __init__(self, **kw):
        self.__dict__.update(kw)

    def emit_moe(self, tag):
        S, carve, X = self.S, self.carve, self.X
        B1_OFF, B2_OFF = self.B1_OFF, self.B2_OFF
        PS_A, PS_B = self.PS_A, self.PS_B
        WBUF = self.WBUF
        gW, io, ioS = self.gW, self.io, self.ioS
        tv, vv, sv = self.tv, self.vv, self.sv
        t, d, w, rows = self.t, self.d, self.w, self.rows
        g_load, io_load, io_store = self.g_load, self.io_load, self.io_store
        onesr, identf = t["onesr"], t["identf"]

        self.emit_stats_and_center(tag)

        xn = [carve(B1_OFF, 4096 * i, f"xn{i}_{tag}", [128, 2048], BF16)
              for i in range(16)]
        R_b = carve(B2_OFF, 0, f"Rb_{tag}", [128, 2048], F32)
        Gb = carve(B2_OFF, 8192, f"Gb_{tag}", [128, 2048], F32)
        hsil = carve(B2_OFF, 16384, f"hsil_{tag}", [128, 2048], BF16)
        hpr = [carve(B2_OFF, 20480 + 4096 * i, f"hpr{i}_{tag}", [128, 2048], BF16)
               for i in range(4)]
        gfm = carve(B2_OFF, 36864, f"gfm_{tag}", [8, 2048], F32)
        grow = carve(B2_OFF, 45056, f"grow_{tag}", [1, 2048], F32)
        sct = carve(B2_OFF, 53248, f"sct_{tag}", [128, 128], F32)
        u = [carve(B2_OFF, 53760 + 512 * i, f"u{i}_{tag}", [128, 128], F32)
             for i in range(6)]
        m12 = carve(B2_OFF, 56832, f"m12_{tag}", [128, 64], F32)
        rrow = carve(B2_OFF, 57344, f"rrow_{tag}", [1, 2048], F32)

        self.tap(f"xcm_{tag}", X[0][:, :])
        io_load(rrow[:, :], d["d_rows"][rows["ROW_R"]:rows["ROW_R"] + 1, :],
                parity=0, waits=[ioS])
        self.emit_row_bcast(rrow, R_b, PS_A, extra_waits=[(io[0], io[0].n)])
        for c in range(16):
            S.op("vector", lambda v, c=c: v.tensor_tensor(
                xn[c][:, :], X[c][:, :], R_b[:, :], op=ALU.mult),
                waits=[vv], inc=(vv if c == 15 else None))
        self.tap(f"xn0_{tag}", xn[0][:, :])

        for tt in range(16):
            for fc in range(16):
                S.op("tensor", lambda tn, tt=tt, fc=fc: tn.matmul(
                    PS_A[:, tt * 8:(tt + 1) * 8],
                    xn[fc][:, tt * 128:(tt + 1) * 128],
                    self.t["rtr_t"][:, fc * 8:(fc + 1) * 8],
                    start=(fc == 0), stop=(fc == 15)),
                    waits=[(vv, vv.n)],
                    inc=(tv if (tt == 15 and fc == 15) else None))
        S.op("vector", lambda v: v.tensor_copy(sct[:, :], PS_A[:, 0:128]),
             waits=[tv], inc=vv)
        self.tap(f"sct_{tag}", sct[:, :], f32=True)
        if DEBUG:
            S.join()
        BIG = 1e8
        sct3 = sct[:, :].rearrange("p (t e) -> p t e", e=8)
        def m1s(v):
            for tt in range(16):
                v.reduce_max(m12[:, tt:tt + 1], sct[:, tt * 8:(tt + 1) * 8], axis=AX.X)
            return v.engine_nop()
        S.op("vector", m1s, inc=vv)
        S.op("vector", lambda v: v.tensor_tensor(
            u[0][:, :].rearrange("p (t e) -> p t e", e=8), sct3,
            m12[:, 0:16].unsqueeze(2).broadcast_to([128, 16, 8]),
            op=ALU.subtract), inc=vv)
        S.op("scalar", lambda s: s.activation(out=u[1][:, :], in_=u[0][:, :],
                                              func=AF.Exp, scale=BIG),
             waits=[vv], inc=sv)
        S.op("vector", lambda v: v.scalar_tensor_tensor(
            u[2][:, :], u[1][:, :], -1e30, sct[:, :], op0=ALU.mult, op1=ALU.add),
            waits=[sv], inc=vv)
        def m2s(v):
            for tt in range(16):
                v.reduce_max(m12[:, 16 + tt:17 + tt], u[2][:, tt * 8:(tt + 1) * 8],
                             axis=AX.X)
            return v.engine_nop()
        S.op("vector", m2s, inc=vv)
        S.op("vector", lambda v: v.tensor_tensor(
            u[3][:, :].rearrange("p (t e) -> p t e", e=8), sct3,
            m12[:, 16:32].unsqueeze(2).broadcast_to([128, 16, 8]),
            op=ALU.subtract))
        S.op("vector", lambda v: v.tensor_scalar_min(u[3][:, :], u[3][:, :], 0.0),
             inc=vv)
        S.op("scalar", lambda s: s.activation(out=u[4][:, :], in_=u[3][:, :],
                                              func=AF.Exp, scale=BIG),
             waits=[vv], inc=sv)
        S.op("vector", lambda v: v.tensor_scalar(u[5][:, :], u[1][:, :], -1.0, 1.0,
                                                 op0=ALU.mult, op1=ALU.add),
             waits=[sv])
        S.op("vector", lambda v: v.tensor_tensor(u[4][:, :], u[4][:, :], u[5][:, :],
                                                 op=ALU.mult))
        S.op("vector", lambda v: v.tensor_tensor(m12[:, 32:48], m12[:, 0:16],
                                                 m12[:, 16:32], op=ALU.subtract),
             inc=vv)
        S.op("scalar", lambda s: s.activation(out=m12[:, 48:64], in_=m12[:, 32:48],
                                              func=AF.Sigmoid), waits=[vv], inc=sv)
        S.op("vector", lambda v: v.tensor_scalar(m12[:, 32:48], m12[:, 48:64],
                                                 -1.0, 1.0, op0=ALU.mult,
                                                 op1=ALU.add), waits=[sv])
        S.op("vector", lambda v: v.tensor_tensor(
            u[1][:, :].rearrange("p (t e) -> p t e", e=8),
            u[1][:, :].rearrange("p (t e) -> p t e", e=8),
            m12[:, 48:64].unsqueeze(2).broadcast_to([128, 16, 8]), op=ALU.mult))
        S.op("vector", lambda v: v.tensor_tensor(
            u[4][:, :].rearrange("p (t e) -> p t e", e=8),
            u[4][:, :].rearrange("p (t e) -> p t e", e=8),
            m12[:, 32:48].unsqueeze(2).broadcast_to([128, 16, 8]), op=ALU.mult))
        S.op("vector", lambda v: v.tensor_tensor(u[0][:, :], u[1][:, :], u[4][:, :],
                                                 op=ALU.add), inc=vv)
        self.tap(f"gt_{tag}", u[0][:, :], f32=True)
        self.tap(f"m12_{tag}", m12[:, :], f32=True)
        if DEBUG:
            S.join()
        for tt in range(16):
            S.op("tensor", lambda tn, tt=tt: tn.transpose(
                PS_B[0:8, tt * 128:(tt + 1) * 128], u[0][:, tt * 8:(tt + 1) * 8],
                identf[:, :]), waits=[(vv, vv.n)],
                inc=(tv if tt == 15 else None))
        S.op("vector", lambda v: v.tensor_copy(gfm[:, :], PS_B[0:8, :]),
             waits=[tv], inc=vv)
        self.tap(f"gfm_{tag}", gfm[:, :], f32=True)
        io_store(d["d_gfm"][:, :], gfm[:, :], waits=[vv])
        GSTORED = ioS.n

        self.hsilm = vv.n
        self.hprm = vv.n
        marks = [tv.n, tv.n]
        wj = 0
        gbm = tv.n
        for e in range(8):
            io_load(grow[:, :], d["d_gfm"][e:e + 1, :], parity=0,
                    waits=[(ioS, GSTORED), (tv, gbm),
                           (io[0], io[0].n), (io[1], io[1].n)])
            self.emit_row_bcast(grow, Gb, PS_B,
                                extra_waits=[(io[0], io[0].n), (vv, vv.n)])
            gbm = tv.n
            self.hsilm = max(self.hsilm, vv.n)
            for ot in range(4):
                par = wj % 2; wj += 1
                g_load(WBUF[par][:, :], w["w1_d"][e, ot], ctr=gW[par],
                       waits=[(tv, marks[par])])
                for fc in range(16):
                    for nb in range(4):
                        S.op("tensor", lambda tn, fc=fc, nb=nb, par=par: tn.matmul(
                            PS_A[:, nb * 512:(nb + 1) * 512],
                            WBUF[par][:, fc * 128:(fc + 1) * 128],
                            xn[fc][:, nb * 512:(nb + 1) * 512],
                            start=(fc == 0), stop=(fc == 15)),
                            waits=([(gW[par], gW[par].n), (sv, sv.n)]
                                   if (fc == 0 and nb == 0) else []),
                            inc=(tv if (fc == 15 and nb == 3) else None))
                marks[par] = tv.n
                S.op("scalar", lambda s: s.activation(out=hsil[:, :], in_=PS_A[:, :],
                                                      func=AF.Silu),
                     waits=[tv, (vv, self.hprm)], inc=sv)
                par = wj % 2; wj += 1
                g_load(WBUF[par][:, :], w["w3_d"][e, ot], ctr=gW[par],
                       waits=[(tv, marks[par])])
                for fc in range(16):
                    for nb in range(4):
                        S.op("tensor", lambda tn, fc=fc, nb=nb, par=par: tn.matmul(
                            PS_B[:, nb * 512:(nb + 1) * 512],
                            WBUF[par][:, fc * 128:(fc + 1) * 128],
                            xn[fc][:, nb * 512:(nb + 1) * 512],
                            start=(fc == 0), stop=(fc == 15)),
                            waits=([(gW[par], gW[par].n), (sv, sv.n),
                                    (vv, self.hsilm)]
                                   if (fc == 0 and nb == 0) else []),
                            inc=(tv if (fc == 15 and nb == 3) else None))
                marks[par] = tv.n
                S.op("vector", lambda v: v.tensor_tensor(hsil[:, :], hsil[:, :],
                                                         PS_B[:, :], op=ALU.mult),
                     waits=[tv, sv])
                self.hsilm = vv.n
                S.op("vector", lambda v, ot=ot: v.tensor_tensor(
                    hpr[ot][:, :], hsil[:, :], Gb[:, :], op=ALU.mult), inc=vv)
                self.hprm = vv.n
                if e == 0 and ot == 0:
                    self.tap(f"hpr0_{tag}", hpr[0][:, :])
                    self.tap(f"Gb0_{tag}", Gb[:, :], f32=True)
                    if DEBUG:
                        S.join()
            w2m = [vv.n, vv.n]
            for dt16 in range(16):
                par = wj % 2; wj += 1
                g_load(WBUF[par][:, 0:512], w["w2_d"][e, dt16], ctr=gW[par],
                       waits=[(tv, marks[par])])
                ps = PS_A if dt16 % 2 == 0 else PS_B
                for fc in range(4):
                    for nb in range(4):
                        S.op("tensor", lambda tn, fc=fc, nb=nb, par=par, ps=ps: tn.matmul(
                            ps[:, nb * 512:(nb + 1) * 512],
                            WBUF[par][:, fc * 128:(fc + 1) * 128],
                            hpr[fc][:, nb * 512:(nb + 1) * 512],
                            start=(fc == 0), stop=(fc == 3)),
                            waits=([(gW[par], gW[par].n), (vv, w2m[dt16 % 2]),
                                    (sv, sv.n)]
                                   if (fc == 0 and nb == 0) else []),
                            inc=(tv if (fc == 3 and nb == 3) else None))
                marks[par] = tv.n
                S.op("vector", lambda v, dt16=dt16, ps=ps: v.tensor_tensor(
                    X[dt16][:, :], X[dt16][:, :], ps[:, :], op=ALU.add),
                    waits=[tv], inc=vv)
                w2m[dt16 % 2] = vv.n
            if e == 0:
                self.tap(f"xe0_{tag}", X[0][:, :])
                if DEBUG:
                    S.join()
        S.join()
        self.tap(f"xout_{tag}", X[0][:, :])

    def emit_mamba(self):
        S, carve, X = self.S, self.carve, self.X
        B1_OFF, B2_OFF, X_OFF = self.B1_OFF, self.B2_OFF, self.X_OFF
        PS_A, PS_B = self.PS_A, self.PS_B
        PS_Abf, PS_Bbf = self.PS_Abf, self.PS_Bbf
        WBUF, OBUF = self.WBUF, self.OBUF
        gW, io, ioT, ioS = self.gW, self.io, self.ioT, self.ioS
        gA = self.gA
        gG = self.gG
        tv, vv, sv = self.tv, self.vv, self.sv
        pv = self.pv
        t, d, w, rows = self.t, self.d, self.w, self.rows
        g_load, io_load, io_store = self.g_load, self.io_load, self.io_store
        onesr, ones64 = t["onesr"], t["ones64"]
        identb, identf = t["identb"], t["identf"]
        maskb = t["maskb"]
        lnw_c, rhoT_t = t["lnw_c"], t["rhoT_t"]
        cw_t, cb_t, dtb_t, alog_t, a_t = (t["cw_t"], t["cb_t"], t["dtb_t"],
                                          t["alog_t"], t["a_t"])
        tap = self.tap

        self.emit_stats_and_center("mb", extra_rows=("rho3", "r1p"))

        XL = [carve(B2_OFF, 4096 * i, f"XL{i}", [128, 2048], BF16) for i in range(16)]
        wgh = [carve(B1_OFF, 4096 * i, f"wgh{i}", [128, 2048], BF16) for i in range(16)]
        for c in range(16):
            S.op("vector", lambda v, c=c: v.tensor_scalar_mul(
                XL[c][:, :], X[c][:, :], lnw_c[:, c:c + 1]),
                inc=(vv if c == 15 else None))
        tap("XL0", XL[0][:, :])
        for oh in range(2):
            for fc in range(16):
                g_load(wgh[fc][:, :], w["wg_d"][oh, fc], ctr=gW[oh],
                       waits=[(tv, tv.n)])
            g1m = [sv.n, sv.n]
            for tt in range(16):
                ps = PS_A if tt % 2 == 0 else PS_B
                for fc in range(16):
                    for nb in range(4):
                        S.op("tensor", lambda tn, fc=fc, nb=nb, tt=tt, ps=ps: tn.matmul(
                            ps[:, nb * 512:(nb + 1) * 512],
                            XL[fc][:, tt * 128:(tt + 1) * 128],
                            wgh[fc][:, nb * 512:(nb + 1) * 512],
                            start=(fc == 0), stop=(fc == 15)),
                            waits=[(gW[oh], gW[oh].n), (vv, vv.n),
                                   (sv, g1m[tt % 2])],
                            inc=(tv if (fc == 15 and nb == 3) else None))
                for half in range(2):
                    ob = OBUF[half]
                    S.op("scalar", lambda s, ps=ps, ob=ob, half=half, tt=tt: s.activation(
                        out=ob[:, :], in_=ps[:, half * 1024:(half + 1) * 1024],
                        func=AF.Silu, scale=rhoT_t[:, tt:tt + 1]),
                        waits=[tv, (ioT[half], ioT[half].n)], inc=sv)
                    io_store(d["d_gate"][tt, oh * 2 + half], ob[:, :],
                             waits=[sv], parity=half)
                g1m[tt % 2] = sv.n
        S.join()

        sA = carve(B1_OFF, 0, "m2_sA", [64, 2048], F32)
        sB = carve(B1_OFF, 8192, "m2_sB", [64, 2048], F32)
        dt_f = carve(B1_OFF, 16384, "m2_dtf", [64, 2048], F32)
        RHO3_b = carve(B1_OFF, 24576, "RHO3b", [128, 2048], F32)
        wdt_t = [carve(B1_OFF, 32768 + 128 * i, f"wdt{i}", [128, 64], BF16)
                 for i in range(16)]
        RAW_BYTES = 34848
        dtT = carve(B1_OFF, RAW_BYTES, "dtT", [128, 1024], F32)
        acumT = carve(B1_OFF, RAW_BYTES + 4096, "acumT", [128, 1024], F32)
        alast_c = carve(B1_OFF, RAW_BYTES + 8192, "alast_c", [64, 16], F32)
        NW_b = carve(B1_OFF, RAW_BYTES + 8288, "NW_b", [128, 4096], BF16)
        aflat = carve(B1_OFF, RAW_BYTES + 16480, "aflat", [1, 2048], F32)
        rho_row = carve(B1_OFF, 16384, "rho_row", [1, 2048], F32)

        io_load(rho_row[:, :], d["d_rows"][rows["ROW_RHO3"]:rows["ROW_RHO3"] + 1, :],
                parity=0, waits=[ioS])
        self.emit_row_bcast(rho_row, RHO3_b, PS_A,
                            extra_waits=[(io[0], io[0].n), (vv, vv.n), (sv, sv.n)])
        for i in range(16):
            g_load(wdt_t[i][:, :], w["wdt_d"][i], ctr=gW[0], waits=[(tv, tv.n)])
        for fc in range(16):
            for nb in range(4):
                S.op("tensor", lambda tn, fc=fc, nb=nb: tn.matmul(
                    PS_B[0:64, nb * 512:(nb + 1) * 512], wdt_t[fc][:, :],
                    XL[fc][:, nb * 512:(nb + 1) * 512],
                    start=(fc == 0), stop=(fc == 15)),
                    waits=[(gW[0], gW[0].n), (vv, vv.n)],
                    inc=(tv if (fc == 15 and nb == 3) else None))
        S.op("vector", lambda v: v.tensor_tensor(sA[:, :], PS_B[0:64, :],
                                                 RHO3_b[0:64, :], op=ALU.mult),
             waits=[tv])
        S.op("scalar", lambda s: s.activation(out=sA[:, :], in_=sA[:, :],
                                              func=AF.Identity,
                                              bias=dtb_t[:, 0:1]),
             waits=[(vv, vv.n)], inc=sv)
        S.op("scalar", lambda s: s.activation(out=sB[:, :], in_=sA[:, :], func=AF.Abs))
        S.op("scalar", lambda s: s.activation(out=sB[:, :], in_=sB[:, :], func=AF.Exp,
                                              scale=-1.0))
        S.op("scalar", lambda s: s.activation(out=sB[:, :], in_=sB[:, :], func=AF.Ln,
                                              bias=1.0), inc=sv)
        S.op("vector", lambda v: v.tensor_scalar_max(dt_f[:, :], sA[:, :], 0.0),
             waits=[sv])
        S.op("vector", lambda v: v.tensor_add(dt_f[:, :], dt_f[:, :], sB[:, :]),
             inc=vv)
        tap("dtf", dt_f[0:64, :], f32=True)
        S.op("scalar", lambda s: s.activation(out=a_t[:, :], in_=alog_t[:, :],
                                              func=AF.Exp), inc=sv)
        S.op("vector", lambda v: v.tensor_scalar_mul(a_t[:, :], a_t[:, :], -1.0),
             waits=[sv])
        S.op("vector", lambda v: v.tensor_scalar_mul(sA[:, :], dt_f[:, :],
                                                     a_t[:, 0:1]))
        z128 = carve(B1_OFF, 32768 + 2048 + 32, "z128", [64, 128], F32)
        S.op("vector", lambda v: v.memset(z128[:, :], 0.0))
        for c in range(16):
            sl = slice(c * 128, (c + 1) * 128)
            S.op("vector", lambda v, sl=sl: v.tensor_tensor_scan(
                sB[:, sl], sA[:, sl], z128[:, :], 0.0, op0=ALU.add, op1=ALU.add))
        ACUM = sB
        tap("acum", ACUM[0:64, :], f32=True)
        for c in range(16):
            S.op("vector", lambda v, c=c: v.tensor_copy(
                alast_c[:, c:c + 1], ACUM[:, c * 128 + 127:c * 128 + 128]),
                inc=(vv if c == 15 else None))
        for c in range(16):
            S.op("tensor", lambda tn, c=c: tn.transpose(
                PS_A[:, 0:64], dt_f[:, c * 128:(c + 1) * 128], identf[0:64, 0:64]),
                waits=[(vv, vv.n)], inc=tv)
            S.op("vector", lambda v, c=c: v.tensor_copy(
                dtT[:, c * 64:(c + 1) * 64], PS_A[:, 0:64]), waits=[tv], inc=vv)
            S.op("tensor", lambda tn, c=c: tn.transpose(
                PS_A[:, 64:128], ACUM[:, c * 128:(c + 1) * 128],
                identf[0:64, 0:64]), waits=[(vv, vv.n)], inc=tv)
            S.op("vector", lambda v, c=c: v.tensor_copy(
                acumT[:, c * 64:(c + 1) * 64], PS_A[:, 64:128]), waits=[tv], inc=vv)
        for c in range(16):
            io_store(d["d_flat"][c], ACUM[:, c * 128:(c + 1) * 128],
                     waits=[(vv, vv.n)])
        nwrow = carve(B1_OFF, 0, "nwrow", [1, 4096], F32)
        io_load(nwrow[:, :], w["nw_d"][:, :], parity=0,
                waits=[(vv, vv.n), (ioS, ioS.n)])
        for h0 in range(2):
            for nb in range(4):
                S.op("tensor", lambda tn, h0=h0, nb=nb: tn.matmul(
                    PS_B[:, nb * 512:(nb + 1) * 512], onesr[:, :],
                    nwrow[0:1, h0 * 2048 + nb * 512:h0 * 2048 + (nb + 1) * 512],
                    start=True, stop=True),
                    waits=[(io[0], io[0].n), (vv, vv.n)],
                    inc=(tv if nb == 3 else None))
            S.op("vector", lambda v, h0=h0: v.tensor_copy(
                NW_b[:, h0 * 2048:(h0 + 1) * 2048], PS_B[:, 0:2048]),
                waits=[tv], inc=vv)
        S.join()

        zpad = carve(B1_OFF, 0, "zpad", [128, 16], BF16)
        S.op("vector", lambda v: v.memset(zpad[:, :], 0.0), inc=vv)
        for ot in range(33):
            io_store(d["d_raw"][ot][:, 0:16], zpad[:, :], waits=[(vv, vv.n)])
        marks = [tv.n, tv.n]
        m3m = [vv.n, vv.n]
        for ot in range(33):
            par = ot % 2
            src = w["wxs_d"][ot] if ot < 32 else w["wbc_d"]
            g_load(WBUF[par][:, :], src, ctr=gW[par], waits=[(tv, marks[par])])
            ps = PS_A if par == 0 else PS_B
            for fc in range(16):
                for nb in range(4):
                    S.op("tensor", lambda tn, fc=fc, nb=nb, par=par, ps=ps: tn.matmul(
                        ps[:, nb * 512:(nb + 1) * 512],
                        WBUF[par][:, fc * 128:(fc + 1) * 128],
                        XL[fc][:, nb * 512:(nb + 1) * 512],
                        start=(fc == 0), stop=(fc == 15)),
                        waits=[(gW[par], gW[par].n), (vv, m3m[par])],
                        inc=(tv if (fc == 15 and nb == 3) else None))
            marks[par] = tv.n
            for half in range(2):
                S.op("vector", lambda v, ps=ps, half=half: v.tensor_tensor(
                    OBUF[half][:, :], ps[:, half * 1024:(half + 1) * 1024],
                    RHO3_b[:, half * 1024:(half + 1) * 1024], op=ALU.mult),
                    waits=[tv, (ioT[half], ioT[half].n)],
                    inc=(vv if half == 1 else None))
                io_store(d["d_raw"][ot][:, 16 + half * 1024:16 + (half + 1) * 1024],
                         OBUF[half][:, :], waits=[(vv, vv.n)], parity=half)
                if half == 1:
                    m3m[par] = vv.n
                if ot == 0:
                    tap(f"m3ob{half}", OBUF[half][:, :])
                    if DEBUG:
                        S.join()
        S.join()

        for c in range(16):
            io_store(d["d_x"][c], X[c][:, :])
        XSP = ioS.n
        S.join()

        XT_c = carve(X_OFF, 0, "XT_c", [128, 4096], BF16)
        y_c = carve(X_OFF, 8192, "y_c", [128, 4096], BF16)
        gate_c = carve(X_OFF, 16384, "gate_c", [128, 4096], BF16)
        D_q = carve(X_OFF, 24576, "D_q", [128, 2048], F32)
        MtL = carve(X_OFF, 32768, "MtL", [128, 2048], BF16)
        Lt = carve(X_OFF, 36864, "Lt", [128, 2048], BF16)
        prev = carve(X_OFF, 40960, "prev", [64, 4096], BF16)
        Bfm = carve(X_OFF, 49152, "Bfm", [64, 2048], BF16)
        Cfm = carve(X_OFF, 53248, "Cfm", [64, 2048], BF16)
        xd_q = carve(X_OFF, 57344, "xd_q", [128, 1024], BF16)
        t1q = carve(X_OFF, 59392, "t1q", [64, 1024], BF16)
        cacc = carve(X_OFF, 61440, "cacc", [128, 512], BF16)
        cacc2 = carve(X_OFF, 64480, "cacc2", [128, 512], BF16)
        cacc3 = carve(B1_OFF, 59520, "cacc3", [128, 512], BF16)
        cacc4 = carve(B1_OFF, 60544, "cacc4", [128, 512], BF16)
        eps_c = carve(X_OFF, 65504, "eps_c", [128, 1], F32)
        Btm = carve(X_OFF, 62464, "Btm", [128, 64], BF16)
        sdT = carve(X_OFF, 62592, "sdT", [128, 64], F32)
        dsT = carve(X_OFF, 62848, "dsT", [128, 64], F32)
        d1t = carve(X_OFF, 63104, "d1t", [128, 64], F32)
        DCOL = carve(X_OFF, 63360, "DCOL", [128, 64], F32)
        Mtm = carve(X_OFF, 63616, "Mtm", [128, 128], BF16)
        alast_r = carve(X_OFF, 63904, "alast_r", [1, 64], F32)
        cdrow = carve(X_OFF, 64160, "cdrow", [1, 64], F32)
        drow_t = carve(X_OFF, 64416, "drow_t", [1, 64], F32)
        raw_w = [carve(B1_OFF, 1056 * i, f"raw{i}", [128, 516], BF16)
                 for i in range(33)]
        yTall = carve(B2_OFF, 0, "yTall", [128, 32768], BF16)
        yT3 = yTall[:, :].rearrange("p (i n) -> p i n", n=1024)

        self.AFM = tv.n
        io_load(drow_t[:, :], w["drow_d"][:, :], parity=0, waits=[(ioS, XSP)])
        S.op("tensor", lambda tn: tn.matmul(PS_A[:, 0:64], onesr[:, :],
                                            drow_t[0:1, :], start=True, stop=True),
             waits=[(io[0], io[0].n), (tv, tv.n)], inc=tv)
        S.op("vector", lambda v: v.tensor_copy(DCOL[:, :], PS_A[:, 0:64]),
             waits=[tv])
        S.op("vector", lambda v: v.memset(prev[:, :], 0.0))
        S.op("vector", lambda v: v.memset(eps_c[:, :], EPS_RMS), inc=vv)
        S.join()

        self.pending_epi = None
        self.SQM = sv.n
        self.YTRM = tv.n
        self.YTM = sv.n
        self.PSBM = vv.n
        self.DMARK = vv.n
        self.TOFFM = vv.n
        self.EXPM = sv.n
        self.MIDM = vv.n
        self.STM = tv.n
        self.pend_AFL = None
        self.XTCM = sv.n
        self.silu_m = [0, 0, 0, 0]
        self.CONVS = sv.n
        self.CONVV = vv.n

        def emit_chunk_epi():
            if self.pending_epi is None:
                return
            c, GATEC = self.pending_epi
            self.pending_epi = None
            sq_scr = D_q.bitcast(BF16)
            S.op("scalar", lambda s: s.activation(
                out=sq_scr[:, 0:4096], in_=y_c[:, :], func=AF.Square,
                accum_out=d1t[:, 0:1]),
                waits=[(vv, vv.n), (pv, pv.n)], inc=sv)
            self.SQM = sv.n
            S.op("scalar", lambda s: s.activation(
                out=d1t[:, 2:3], in_=d1t[:, 0:1], func=AF.Sqrt,
                scale=1.0 / 4096.0, bias=eps_c[:, 0:1]), inc=sv)
            S.op("vector", lambda v: v.reciprocal(d1t[:, 3:4], d1t[:, 2:3]),
                 waits=[(sv, sv.n)])
            S.op("vector", lambda v: v.scalar_tensor_tensor(
                y_c[:, :], y_c[:, :], d1t[:, 3:4], NW_b[:, :],
                op0=ALU.mult, op1=ALU.mult), inc=vv)
            if c == 0:
                tap("yc0", y_c[:, 0:2048])
                tap("ycB", y_c[:, 2048:4096])
                if DEBUG:
                    S.join()
            for ic in range(32):
                S.op("tensor", lambda tn, ic=ic: tn.transpose(
                    PS_Bbf[:, ic * 128:(ic + 1) * 128],
                    y_c[:, ic * 128:(ic + 1) * 128], identb[:, :]),
                    waits=([(vv, vv.n), (sv, self.XTCM)] if ic == 0 else []),
                    inc=(tv if ic == 31 else None))
            self.YTRM = tv.n
            blk = c % 8
            S.op("scalar", lambda s, blk=blk: s.activation(
                out=yT3[:, :, blk * 128:(blk + 1) * 128],
                in_=PS_Bbf[:, 0:4096].rearrange("p (i n) -> p i n", n=128),
                func=AF.Copy), waits=[tv], inc=sv)
            self.YTM = sv.n

        def aflat_load(c, q):
            S.op("scalar", lambda g, c=c, q=q: g.dma_start(
                out=aflat[:, :],
                in_=d["d_flat"][c].rearrange("a b -> (a b)")
                .unsqueeze(0)[0:1, q * 2048:(q + 1) * 2048]),
                waits=[(tv, self.AFM)], inc=gA, inc_k=16, no_own=True)
            self.pend_AFL = gA.n

        def emit_A(c, q, extra=()):
            if self.pend_AFL is None:
                aflat_load(c, q)
            AFL = self.pend_AFL
            self.pend_AFL = None
            for nb in range(4):
                S.op("tensor", lambda tn, nb=nb: tn.matmul(
                    PS_A[:, nb * 512:(nb + 1) * 512], onesr[:, :],
                    aflat[0:1, nb * 512:(nb + 1) * 512],
                    start=True, stop=True),
                    waits=(([(gA, AFL), (vv, self.DMARK)]
                            + list(extra)) if nb == 0 else []),
                    inc=(tv if nb == 3 else None))
            self.AFM = tv.n
            S.op("vector", lambda v, c=c, q=q: v.tensor_tensor(
                D_q[:, :].rearrange("p (h l) -> p h l", h=16),
                acumT[:, c * 64 + q * 16:c * 64 + (q + 1) * 16]
                .unsqueeze(2).broadcast_to([128, 16, 128]),
                PS_A[:, :].rearrange("p (h l) -> p h l", h=16),
                op=ALU.subtract),
                waits=[tv, (sv, sv.n)], inc=vv)
            self.DMARK = vv.n
            S.op("scalar", lambda s: s.activation(
                out=D_q[:, :], in_=D_q[:, :], func=AF.Relu),
                waits=[(vv, vv.n)])
            S.op("scalar", lambda s: s.activation(
                out=Lt[:, :], in_=D_q[:, :], func=AF.Exp, scale=-1.0),
                inc=sv)
            self.EXPM = sv.n

        def emit_Mid(c, q):
            S.op("vector", lambda v: v.tensor_tensor(
                MtL[:, :].rearrange("p (h l) -> p h l", h=16),
                Lt[:, :].rearrange("p (h l) -> p h l", h=16),
                Mtm[:, :].unsqueeze(1).broadcast_to([128, 16, 128]),
                op=ALU.mult),
                waits=[(sv, max(self.EXPM, self.XTCM)), (tv, self.STM)],
                inc=vv)
            S.op("vector", lambda v, c=c, q=q: v.tensor_tensor(
                xd_q[:, :].rearrange("p (h e) -> p h e", h=16),
                XT_c[:, q * 1024:(q + 1) * 1024]
                .rearrange("p (h e) -> p h e", h=16),
                dtT[:, c * 64 + q * 16:c * 64 + (q + 1) * 16]
                .unsqueeze(2).broadcast_to([128, 16, 64]),
                op=ALU.mult), inc=vv)
            self.MIDM = vv.n

        def emit_iter(c, q, nxt):
            qc = slice(q * 16 * 64, (q + 1) * 16 * 64)
            if nxt is not None:
                emit_A(*nxt)
            for j in range(16):
                S.op("tensor", lambda tn, j=j: tn.matmul(
                    PS_B[:, j * 64:(j + 1) * 64],
                    MtL[:, j * 128:(j + 1) * 128],
                    xd_q[:, j * 64:(j + 1) * 64],
                    start=True, stop=True),
                    waits=([(vv, max(self.PSBM, self.MIDM)), (sv, self.YTM)]
                           if j == 0 else []),
                    inc=(tv if j == 15 else None))
            for j in range(16):
                S.op("tensor", lambda tn, j=j, c=c, q=q: tn.matmul(
                    PS_B[:, 1024 + j * 64:1024 + (j + 1) * 64],
                    Cfm[:, c * 128:(c + 1) * 128],
                    prev[:, (q * 16 + j) * 64:(q * 16 + j + 1) * 64],
                    start=True, stop=True),
                    inc=(tv if j == 15 else None))
            S.op("vector", lambda v, q=q: v.tensor_tensor(
                xd_q[:, :].rearrange("p (h e) -> p h e", h=16),
                xd_q[:, :].rearrange("p (h e) -> p h e", h=16),
                dsT[:, q * 16:(q + 1) * 16]
                .unsqueeze(2).broadcast_to([128, 16, 64]),
                op=ALU.mult), waits=[tv], inc=vv)
            S.op("vector", lambda v, q=q: v.tensor_tensor(
                MtL[:, 1024:2048].rearrange("p (h e) -> p h e", h=16),
                PS_B[:, 1024:2048].rearrange("p (h e) -> p h e", h=16),
                sdT[:, q * 16:(q + 1) * 16]
                .unsqueeze(2).broadcast_to([128, 16, 64]),
                op=ALU.mult), inc=vv)
            self.TOFFM = vv.n
            S.op("vector", lambda v: v.tensor_tensor(
                MtL[:, 0:1024], PS_B[:, 0:1024], MtL[:, 1024:2048],
                op=ALU.add), inc=vv)
            T2M = vv.n
            for j in range(16):
                S.op("tensor", lambda tn, j=j: tn.matmul(
                    PS_B[0:64, j * 64:(j + 1) * 64], Btm[:, :],
                    xd_q[:, j * 64:(j + 1) * 64],
                    start=True, stop=True),
                    waits=([(vv, T2M)] if j == 0 else []),
                    inc=(tv if j == 15 else None))
            S.op("tensor", lambda tn: tn.matmul(
                PS_B[0:64, 1024:1088], ones64[:, :], cdrow[0:1, :],
                start=True, stop=True), inc=tv)
            self.STM = tv.n
            S.op("vector", lambda v, q=q: v.tensor_tensor(
                MtL[:, 1024:2048].rearrange("p (h e) -> p h e", h=16),
                XT_c[:, q * 1024:(q + 1) * 1024]
                .rearrange("p (h e) -> p h e", h=16),
                DCOL[:, q * 16:(q + 1) * 16]
                .unsqueeze(2).broadcast_to([128, 16, 64]),
                op=ALU.mult), inc=vv)
            S.op("vector", lambda v, q=q: v.tensor_tensor(
                y_c[:, q * 1024:(q + 1) * 1024], MtL[:, 0:1024],
                MtL[:, 1024:2048], op=ALU.add),
                waits=[(tv, self.YTRM)], inc=vv)
            S.op("gpsimd", lambda g, q=q: g.tensor_tensor(
                y_c[:, q * 1024:(q + 1) * 1024],
                y_c[:, q * 1024:(q + 1) * 1024],
                gate_c[:, q * 1024:(q + 1) * 1024], op=ALU.mult),
                waits=[(gG, self.GATEC), (vv, vv.n)], inc=pv)
            if nxt is not None and nxt[0] == c:
                emit_Mid(*nxt)
            S.op("vector", lambda v, q=q: v.tensor_tensor(
                t1q[:, :].rearrange("p (h e) -> p h e", h=16),
                prev[:, qc].rearrange("p (h e) -> p h e", h=16),
                PS_B[0:64, 1024 + q * 16:1024 + (q + 1) * 16]
                .unsqueeze(2).broadcast_to([64, 16, 64]),
                op=ALU.mult), waits=[tv], inc=vv)
            S.op("vector", lambda v: v.tensor_tensor(
                prev[:, qc], t1q[:, :], PS_B[0:64, 0:1024],
                op=ALU.add), inc=vv)
            self.PSBM = vv.n

        self.pend_IOW = None
        for half in range(2):
            for wi in range(2):
                win = half * 2 + wi
                if self.pend_IOW is None:
                    for ot in range(33):
                        io_load(raw_w[ot][:, 0:515],
                                d["d_raw"][ot][:, 13 + 512 * win:13 + 512 * win + 515],
                                parity=ot % 2,
                                waits=[(tv, tv.n), (vv, vv.n), (sv, sv.n)])
                    self.pend_IOW = (io[0].n, io[1].n)
                IOW = self.pend_IOW
                self.pend_IOW = None
                if win == 0:
                    tap("rawpre", raw_w[0][:, 0:516])
                    if DEBUG:
                        S.join()
                cacc_b = [cacc, cacc2, cacc3, cacc4]
                for ot in range(33):
                    ca = cacc_b[ot % 4]
                    S.op("vector", lambda v, ot=ot, ca=ca: v.tensor_scalar_mul(
                        ca[:, :], raw_w[ot][:, 3:515],
                        cw_t[:, ot * 4 + 3:ot * 4 + 4]),
                        waits=[(io[0], IOW[0]), (io[1], IOW[1]),
                               (sv, self.silu_m[ot % 4])])
                    for k in range(3):
                        S.op("vector", lambda v, ot=ot, k=k, ca=ca:
                             v.scalar_tensor_tensor(
                                 ca[:, :], raw_w[ot][:, k:512 + k],
                                 cw_t[:, ot * 4 + k:ot * 4 + k + 1], ca[:, :],
                                 op0=ALU.mult, op1=ALU.add))
                    S.op("scalar", lambda s, ot=ot, ca=ca: s.activation(
                        out=raw_w[ot][:, 3:515], in_=ca[:, :], func=AF.Silu,
                        bias=cb_t[:, ot:ot + 1]), waits=[(vv, vv.n)], inc=sv)
                    self.silu_m[ot % 4] = sv.n
                S.op("vector", lambda v, win=win: v.tensor_copy(
                    Bfm[:, 512 * win:512 * win + 512], raw_w[32][0:64, 3:515]),
                    waits=[(sv, sv.n)])
                S.op("vector", lambda v, win=win: v.tensor_copy(
                    Cfm[:, 512 * win:512 * win + 512], raw_w[32][64:128, 3:515]),
                    inc=vv)
                self.CONVS = sv.n
                self.CONVV = vv.n
                if win == 0:
                    tap("rawc0", raw_w[0][:, 0:516])
                    tap("BCfm", Bfm[0:64, :])
                    if DEBUG:
                        S.join()

                for cc in range(4):
                    c = win * 4 + cc
                    if c == half * 8:
                        emit_A(c, 0, extra=[(sv, sv.n)])
                    for ot in range(32):
                        S.op("tensor", lambda tn, ot=ot, cc=cc: tn.transpose(
                            PS_Bbf[:, ot * 128:(ot + 1) * 128],
                            raw_w[ot][:, 3 + cc * 128:3 + cc * 128 + 128],
                            identb[:, :]),
                            waits=([(sv, max(self.CONVS, self.YTM)),
                                    (vv, self.PSBM)]
                                   if ot == 0 else []),
                            inc=(tv if ot == 31 else None))
                    S.op("scalar", lambda s: s.activation(
                        out=XT_c[:, :], in_=PS_Bbf[:, 0:4096], func=AF.Copy),
                        waits=[tv], inc=sv)
                    self.XTCM = sv.n
                    emit_chunk_epi()
                    S.op("tensor", lambda tn, c=c: tn.transpose(
                        PS_Abf[:, 0:64], Bfm[:, c * 128:(c + 1) * 128],
                        identb[0:64, 0:64]),
                        waits=[(vv, max(self.DMARK, self.CONVV))], inc=tv)
                    S.op("vector", lambda v: v.tensor_copy(Btm[:, :],
                                                           PS_Abf[:, 0:64]),
                         waits=[tv], inc=vv)
                    S.op("tensor", lambda tn, c=c: tn.matmul(
                        PS_A[:, 128:256], Bfm[:, c * 128:(c + 1) * 128],
                        Cfm[:, c * 128:(c + 1) * 128], start=True, stop=True),
                        inc=tv)
                    S.op("vector", lambda v: v.tensor_tensor(
                        Mtm[:, :], PS_A[:, 128:256], maskb[:, :], op=ALU.mult),
                        waits=[tv], inc=vv)
                    S.op("tensor", lambda tn, c=c: tn.transpose(
                        PS_A[0:1, 64:128], alast_c[:, c:c + 1], identf[0:64, 0:64]),
                        inc=tv)
                    S.op("vector", lambda v: v.tensor_copy(alast_r[:, :],
                                                           PS_A[0:1, 64:128]),
                         waits=[tv], inc=vv)
                    S.op("scalar", lambda s: s.activation(
                        out=cdrow[:, :], in_=alast_r[:, :], func=AF.Exp),
                        waits=[(vv, vv.n)], inc=sv)
                    S.op("tensor", lambda tn: tn.matmul(
                        PS_A[:, 0:64], onesr[:, :], alast_r[0:1, :],
                        start=True, stop=True), waits=[(vv, vv.n)], inc=tv)
                    S.op("vector", lambda v, c=c: v.tensor_tensor(
                        d1t[:, :], PS_A[:, 0:64], acumT[:, c * 64:(c + 1) * 64],
                        op=ALU.subtract), waits=[tv], inc=vv)
                    S.op("scalar", lambda s: s.activation(
                        out=dsT[:, :], in_=d1t[:, :], func=AF.Exp),
                        waits=[(vv, vv.n)], inc=sv)
                    S.op("scalar", lambda s, c=c: s.activation(
                        out=sdT[:, :], in_=acumT[:, c * 64:(c + 1) * 64],
                        func=AF.Exp), waits=[(vv, self.TOFFM)], inc=sv)
                    S.op("sync", lambda g, c=c: g.dma_start(
                        out=gate_c[:, :].rearrange("p (q n) -> p q n", q=4),
                        in_=d["d_gate"][c].rearrange("q p n -> p q n")),
                        waits=[(vv, vv.n), (sv, sv.n), (pv, pv.n)],
                        inc=gG, inc_k=16)
                    GATEC = gG.n
                    self.GATEC = GATEC
                    if cc == 3 and win < 3:
                        nwin = win + 1
                        for ot in range(33):
                            io_load(raw_w[ot][:, 0:515],
                                    d["d_raw"][ot]
                                    [:, 13 + 512 * nwin:13 + 512 * nwin + 515],
                                    parity=ot % 2, waits=[(tv, tv.n)])
                        self.pend_IOW = (io[0].n, io[1].n)
                    self.DMARK = max(self.DMARK, vv.n)
                    emit_Mid(c, 0)
                    for q in range(4):
                        if q < 3:
                            nxt = (c, q + 1)
                        elif (c + 1) % 8 != 0:
                            nxt = (c + 1, 0)
                        else:
                            nxt = None
                        emit_iter(c, q, nxt)
                    self.pending_epi = (c, GATEC)
                    if cc == 3 and wi == 1:
                        if half == 0:
                            aflat_load(8, 0)
                        emit_chunk_epi()

            wopb = [carve(X_OFF, 8192, f"wopb0_{half}", [128, 4096], BF16),
                    carve(X_OFF, 16384, f"wopb1_{half}", [128, 4096], BF16)]
            marks5 = [tv.n, tv.n]
            for ot in range(16):
                par = ot % 2
                g_load(wopb[par][:, :], w["wop_d"][ot], ctr=gW[par],
                       waits=[(tv, marks5[par])])
                ps = PS_A if par == 0 else PS_B
                for ic in range(32):
                    for nb in range(2):
                        S.op("tensor", lambda tn, ic=ic, nb=nb, par=par, ps=ps: tn.matmul(
                            ps[:, nb * 512:(nb + 1) * 512],
                            wopb[par][:, ic * 128:(ic + 1) * 128],
                            yTall[:, ic * 1024 + nb * 512:ic * 1024 + (nb + 1) * 512],
                            start=(ic == 0), stop=(ic == 31)),
                            waits=[(gW[par], gW[par].n), (sv, sv.n)],
                            inc=(tv if (ic == 31 and nb == 1) else None))
                marks5[par] = tv.n
                S.op("scalar", lambda s, ps=ps, par=par: s.activation(
                    out=OBUF[par][:, :], in_=ps[:, 0:1024], func=AF.Copy),
                    waits=[tv, (ioT[par], ioT[par].n)], inc=sv)
                io_store(d["d_om"][ot][:, half * 1024:(half + 1) * 1024],
                         OBUF[par][:, :], waits=[(sv, sv.n)], parity=par)
            S.join()

        r1prow = carve(B1_OFF, 12288, "r1prow", [1, 2048], F32)
        omt = [carve(B1_OFF, 0, "omt0", [128, 2048], BF16),
               carve(B1_OFF, 4096, "omt1", [128, 2048], BF16)]
        tres = carve(B1_OFF, 8192, "tres", [128, 2048], BF16)
        io_load(r1prow[:, :], d["d_rows"][rows["ROW_R1P"]:rows["ROW_R1P"] + 1, :],
                parity=0, waits=[(ioS, ioS.n), (ioT[0], ioT[0].n),
                                 (ioT[1], ioT[1].n)])
        for nb in range(4):
            S.op("tensor", lambda tn, nb=nb: tn.matmul(
                PS_A[:, nb * 512:(nb + 1) * 512], onesr[:, :],
                r1prow[0:1, nb * 512:(nb + 1) * 512], start=True, stop=True),
                waits=[(io[0], io[0].n), (tv, tv.n)],
                inc=(tv if nb == 3 else None))
        vmarks6 = [vv.n, vv.n]
        for ot in range(16):
            par = ot % 2
            io_load(X[ot][:, :], d["d_x"][ot], parity=par,
                    waits=[(tv, tv.n), (vv, vmarks6[par])])
            io_load(omt[par][:, :], d["d_om"][ot], parity=par,
                    waits=[(vv, vmarks6[par])])
            S.op("vector", lambda v, ot=ot, par=par: v.tensor_tensor(
                tres[:, :], X[ot][:, :], PS_A[:, :], op=ALU.mult),
                waits=[(io[par], io[par].n)])
            S.op("vector", lambda v, ot=ot, par=par: v.tensor_tensor(
                X[ot][:, :], tres[:, :], omt[par][:, :], op=ALU.add), inc=vv)
            vmarks6[par] = vv.n
        S.join()
        tap("xout_mamba", X[0][:, :])




def _stage_weights(inputs):
    w = {}
    wqkv = inputs["L0_attn_a_w_qkv"]
    b_qkv = inputs["L0_attn_a_b_qkv"]

    def lhsT_blk(W):
        O, F_ = W.shape
        return np.ascontiguousarray(
            W.reshape(O // 128, 128, F_ // 128, 128).transpose(0, 3, 2, 1)
            .reshape(O // 128, 128, F_))

    w["wq_blk"] = lhsT_blk(wqkv[0:2048])
    w["wk_blk"] = lhsT_blk(wqkv[2048:4096])
    wv = wqkv[4096:6144]
    w["wv_blk"] = np.ascontiguousarray(
        wv.reshape(2, 1024, 16, 128).transpose(0, 2, 3, 1))
    w["bq_cols"] = np.ascontiguousarray(b_qkv[0:2048].reshape(16, 128).T)
    w["bk_cols"] = np.ascontiguousarray(b_qkv[2048:4096].reshape(16, 128).T)
    w["bv_cols"] = np.ascontiguousarray(b_qkv[4096:6144].reshape(16, 128).T)
    w["wout_blk"] = lhsT_blk(inputs["L0_attn_a_w_out"])
    w["bout_cols"] = np.ascontiguousarray(inputs["L0_attn_a_b_out"].reshape(16, 128).T)

    rt = inputs["L0_ffn_router_w"].T
    w["router_wT"] = np.ascontiguousarray(
        rt.reshape(16, 128, 8).transpose(1, 0, 2).reshape(128, 128))
    w["w1_blk"] = np.ascontiguousarray(
        inputs["L0_ffn_w1"].reshape(8, 16, 128, 4, 128).transpose(0, 3, 2, 1, 4)
        .reshape(8, 4, 128, 2048))
    w["w3_blk"] = np.ascontiguousarray(
        inputs["L0_ffn_w3"].reshape(8, 16, 128, 4, 128).transpose(0, 3, 2, 1, 4)
        .reshape(8, 4, 128, 2048))
    w["w2_blk"] = np.ascontiguousarray(
        inputs["L0_ffn_w2"].reshape(8, 4, 128, 16, 128).transpose(0, 3, 2, 1, 4)
        .reshape(8, 16, 128, 512))

    wip = inputs["L0_attn_b_in_proj_w"]
    w["wg_blk"] = np.ascontiguousarray(
        wip[0:4096].reshape(2, 2048, 16, 128).transpose(0, 2, 3, 1))
    w["wxs_blk"] = lhsT_blk(wip[4096:8192])
    w["wbc_blk"] = np.ascontiguousarray(
        wip[8192:8320].reshape(128, 16, 128).transpose(2, 1, 0).reshape(128, 2048))
    w["wdt_blk"] = np.ascontiguousarray(
        wip[8320:8384].reshape(64, 16, 128).transpose(1, 2, 0))
    w["convw_cols"] = np.ascontiguousarray(
        inputs["L0_attn_b_conv_w"].reshape(33, 128, 4).transpose(1, 0, 2)
        .reshape(128, 132))
    w["convb_cols"] = np.ascontiguousarray(
        inputs["L0_attn_b_conv_b"].reshape(33, 128).T)
    w["dtbias_col"] = np.ascontiguousarray(inputs["L0_attn_b_dt_bias"].reshape(64, 1))
    w["alog_col"] = np.ascontiguousarray(inputs["L0_attn_b_A_log"].reshape(64, 1))
    w["d_row"] = np.ascontiguousarray(inputs["L0_attn_b_D"][None, :])
    w["nw_row"] = np.ascontiguousarray(inputs["L0_attn_b_norm_weight"][None, :])
    w["lnw_cols"] = np.ascontiguousarray(
        inputs["L0_attn_b_ln_weight"].reshape(16, 128).T)
    w["wop_blk"] = np.ascontiguousarray(
        inputs["L0_attn_b_out_proj_w"].reshape(16, 128, 32, 128)
        .transpose(0, 3, 2, 1).reshape(16, 128, 4096))
    return {k: np.ascontiguousarray(v, dtype=np.float32) for k, v in w.items()}


def kernel(**inputs):
    hs = np.asarray(inputs["L0_attn_a_hidden_states"], dtype=np.float32)
    wmap = _stage_weights({k: np.asarray(v, dtype=np.float32)
                           for k, v in inputs.items()
                           if k != "L0_attn_a_hidden_states"})
    nc = build_nc()
    in_maps = []
    for b in range(NCORES):
        m = dict(wmap)
        m["x_fm"] = np.ascontiguousarray(hs[b].T).reshape(NC, 128, T)
        in_maps.append(m)
    res = run_bass_kernel_spmd(nc, in_maps, core_ids=list(range(NCORES)))
    kernel._last = res
    outs = []
    for b in range(NCORES):
        o = res.results[b]["out_fm"].reshape(H, T)
        outs.append(np.ascontiguousarray(o.T))
    return np.stack(outs, axis=0)


if __name__ == "__main__":
    print("kernel module ok")


# version 32
# speedup vs baseline: 1.0277x; 1.0277x over previous
import sys
sys.path.insert(0, '/opt/trn_rl_repo')
import math
import numpy as np
import ml_dtypes

import concourse.bass as bass
import concourse.mybir as mybir
from concourse.bass_utils import run_bass_kernel_spmd

F32 = mybir.dt.float32
BF16 = mybir.dt.bfloat16
AF = mybir.ActivationFunctionType
ALU = mybir.AluOpType
AX = mybir.AxisListType
BF = ml_dtypes.bfloat16

NCORES = 8
T = 2048
H = 2048
NC = 16
EPS_LN = 1e-5
EPS_RMS = 1e-5
NCHUNK = 16

DEBUG = False
DBG_NAMES = []

DO_ATTN = True
DO_MOE1 = True
DO_MAMBA = True
DO_MOE2 = True


class Ctr:
    def __init__(self, h, name):
        self.h = h
        self.name = name
        self.n = 0


class Sched:
    ENGINES = ("tensor", "vector", "scalar", "sync", "gpsimd")

    def __init__(self):
        self.events = []
        self.all_ctrs = []
        self.own_ctr = {}

    def op(self, engine, fn, waits=(), inc=None, inc_k=1, no_own=False):
        wl = []
        for w in waits:
            if isinstance(w, tuple):
                c, v = w
            else:
                c, v = w, w.n
            assert v <= c.n, f"wait on future value of {c.name}: {v} > {c.n}"
            if v > 0:
                wl.append((c, v))
        own = None if no_own else self.own_ctr.get(engine)
        if own is not None and own.n > 0:
            wl.append((own, own.n))
        incs = []
        if inc is not None:
            incs.append((inc, inc_k))
            inc.n += inc_k
        if own is not None and not any(c is own for c, _ in incs):
            incs.append((own, 1))
            own.n += 1
        self.events.append((engine, fn, wl, incs))

    def join(self, engines=None):
        waits = [(c, c.n) for c in self.all_ctrs if c.n > 0]
        for e in (engines or self.ENGINES):
            self.events.append((e, None, list(waits), []))

    def emit(self, block):
        for eng_name in self.ENGINES:
            def body(eng, eng_name=eng_name):
                lastw = {}
                for engine, fn, waits, incs in self.events:
                    if engine != eng_name:
                        continue
                    for c, v in waits:
                        if lastw.get(c.name, -1) >= v:
                            continue
                        eng.wait_ge(c.h, v)
                        lastw[c.name] = v
                    if fn is None:
                        continue
                    inst = fn(eng)
                    for c, k in incs:
                        assert inst is not None
                        inst.then_inc(c.h, k)
            getattr(block, eng_name)(body)


def build_nc():
    nc = bass.Bass("TRN2", target_bir_lowering=False, debug=False)

    def din(name, shape):
        return nc.dram_tensor(name, list(shape), F32, kind="ExternalInput").ap()

    x_in = din("x_fm", [NC, 128, T])
    wq_d = din("wq_blk", [16, 128, 2048])
    wk_d = din("wk_blk", [16, 128, 2048])
    wv_d = din("wv_blk", [2, 16, 128, 1024])
    bq_d = din("bq_cols", [128, 16])
    bk_d = din("bk_cols", [128, 16])
    bv_d = din("bv_cols", [128, 16])
    wout_d = din("wout_blk", [16, 128, 2048])
    bout_d = din("bout_cols", [128, 16])
    wrtr_d = din("router_wT", [128, 128])
    w1_d = din("w1_blk", [8, 4, 128, 2048])
    w3_d = din("w3_blk", [8, 4, 128, 2048])
    w2_d = din("w2_blk", [8, 16, 128, 512])
    wg_d = din("wg_blk", [2, 16, 128, 2048])
    wxs_d = din("wxs_blk", [32, 128, 2048])
    wbc_d = din("wbc_blk", [128, 2048])
    wdt_d = din("wdt_blk", [16, 128, 64])
    cw_d = din("convw_cols", [128, 132])
    cb_d = din("convb_cols", [128, 33])
    dtb_d = din("dtbias_col", [64, 1])
    alog_d = din("alog_col", [64, 1])
    drow_d = din("d_row", [1, 64])
    nw_d = din("nw_row", [1, 4096])
    lnw_d = din("lnw_cols", [128, 16])
    wop_d = din("wop_blk", [16, 128, 4096])

    out_d = nc.dram_tensor("out_fm", [NC, 128, T], F32, kind="ExternalOutput").ap()
    global DBG_NAMES
    DBG_NAMES = []
    dbg_d = (nc.dram_tensor("dbg", [32, 128, 2048], BF16, kind="ExternalOutput").ap()
             if DEBUG else None)
    dbgf_d = (nc.dram_tensor("dbgf", [6, 128, 2048], F32, kind="ExternalOutput").ap()
              if DEBUG else None)

    def scr(name, shape, dt=BF16):
        return nc.dram_tensor(name, list(shape), dt, kind="Internal").ap()

    d_x = scr("scr_x", [NC, 128, T])
    d_v = scr("scr_v", [16, 128, 2048])
    d_pT = scr("scr_pT", [16, 128, 2048])
    d_gate = scr("scr_gate", [16, 4, 128, 1024])
    d_raw = scr("scr_raw", [33, 128, 2064])
    d_rows = scr("scr_rows", [8, 2048], F32)
    ROW_MUCUM, ROW_R, ROW_RHO3, ROW_R1P, ROW_RSC = 0, 1, 2, 3, 4
    d_flat = scr("scr_flat", [NCHUNK, 64, 128], F32)
    d_gfm = scr("scr_gfm", [8, 2048], F32)
    d_om = scr("scr_om", [16, 128, 2048])

    ones_col_d = nc.inline_tensor(np.ones((128, 1), np.float32), name="c_onescol").ap()
    ones_colb_d = nc.inline_tensor(np.ones((128, 1), dtype=BF), name="c_onescolb").ap()
    ones_row_d = nc.inline_tensor(np.ones((1, 128), np.float32), name="c_onesrow").ap()
    ident_bf_d = nc.inline_tensor(np.eye(128, dtype=BF), name="c_identbf").ap()
    ident_f_d = nc.inline_tensor(np.eye(128, dtype=np.float32), name="c_identf").ap()
    mask_np = (np.arange(128)[None, :] >= np.arange(128)[:, None]).astype(BF)
    mask_bf_d = nc.inline_tensor(mask_np, name="c_mask").ap()
    ones64_d = nc.inline_tensor(np.ones((1, 64), np.float32), name="c_ones64").ap()

    BASE = ((nc.sbuf_base + 31) // 32) * 32
    TOP = nc.sbuf_top

    def nbytes_of(shape, dt):
        per = int(np.prod(shape[1:])) if len(shape) > 1 else 1
        return per * (2 if dt == BF16 else 4)

    cursor = [BASE]

    def place(name, shape, dt):
        at = cursor[0]
        tns = nc.alloc_sbuf_tensor_at(name, list(shape), dt, offset=at)
        cursor[0] = at + ((nbytes_of(shape, dt) + 31) // 32) * 32
        assert cursor[0] <= TOP, f"sbuf overflow placing {name}"
        return tns

    X = [place(f"xt{c}", [128, T], BF16) for c in range(NC)]
    B1_OFF = cursor[0]
    place("big1", [128, 32768], BF16)
    B2_OFF = cursor[0]
    place("big2", [128, 32768], BF16)
    WBUF = [place(f"wbuf{i}", [128, 2048], BF16) for i in range(2)]
    OBUF = [place(f"obuf{i}", [128, 1024], BF16) for i in range(2)]
    onesc = place("onesc", [128, 1], F32)
    onesb2 = place("onesb2", [128, 1], BF16)
    onesr = place("onesr", [1, 128], F32)
    ones64 = place("ones64", [1, 64], F32)
    identb = place("identb", [128, 128], BF16)
    identf = place("identf", [128, 128], F32)
    maskb = place("maskb", [128, 128], BF16)
    bq_c = place("bq_c", [128, 16], F32)
    bk_c = place("bk_c", [128, 16], F32)
    bv_c = place("bv_c", [128, 16], F32)
    bout_c = place("bout_c", [128, 16], F32)
    lnw_c = place("lnw_c", [128, 16], F32)
    ilnw_c = place("ilnw_c", [128, 16], F32)
    rT_t = place("rT_t", [128, 16], F32)
    rscT_t = place("rscT_t", [128, 16], F32)
    rhoT_t = place("rhoT_t", [128, 16], F32)
    rtr_t = place("rtr_t", [128, 128], BF16)
    cw_t = place("cw_t", [128, 132], F32)
    cb_t = place("cb_t", [128, 33], F32)
    dtb_t = place("dtb_t", [64, 1], F32)
    alog_t = place("alog_t", [64, 1], F32)
    a_t = place("a_t", [64, 1], F32)
    nmax_t = place("nmax_t", [128, 8], F32)
    X_OFF = BASE
    SBUF_SMALL_END = cursor[0]
    assert SBUF_SMALL_END <= TOP

    def carve(base_off, byte_off, name, shape, dt):
        assert byte_off + nbytes_of(shape, dt) <= 65536, f"carve overflow {name}"
        return nc.alloc_sbuf_tensor_at(name, list(shape), dt, offset=base_off + byte_off)

    PS_A = nc.alloc_psum_tensor("psA", [128, 2048], F32)
    PS_B = nc.alloc_psum_tensor("psB", [128, 2048], F32)
    PS_Abf = PS_A.bitcast(BF16)
    PS_Bbf = PS_B.bitcast(BF16)

    S = Sched()

    def mkctr(name):
        c = Ctr(nc.alloc_semaphore(name), name)
        S.all_ctrs.append(c)
        return c

    gW = [mkctr("gW0"), mkctr("gW1")]
    gM = mkctr("gM")
    io = [mkctr("io0"), mkctr("io1")]
    ioT = [mkctr("ioT0"), mkctr("ioT1")]
    ioS = mkctr("ioS")
    gA = mkctr("gA")
    gG = mkctr("gG")
    tv = mkctr("tv")
    vv = mkctr("vv")
    sv = mkctr("sv")
    pv = mkctr("pv")
    S.own_ctr = {"vector": vv, "scalar": sv}

    def g_load(dst_ap, src_ap, ctr=None, waits=()):
        c = ctr if ctr is not None else gM
        S.op("gpsimd", lambda g, d=dst_ap, s=src_ap: g.dma_start(out=d, in_=s),
             waits=waits, inc=c, inc_k=16)

    def io_load(dst_ap, src_ap, parity=0, waits=(), slow=False):
        def fn(g, d=dst_ap, s=src_ap, slow=slow):
            return g.dma_start(out=d, in_=s, allow_slow_non_contiguous=slow)
        S.op("sync", fn, waits=waits, inc=io[parity], inc_k=16)

    def io_store(dst_ap, src_ap, waits=(), parity=None):
        c = ioS if parity is None else ioT[parity]
        S.op("sync", lambda g, d=dst_ap, s=src_ap: g.dma_start(out=d, in_=s),
             waits=waits, inc=c, inc_k=16)

    def tap(name, src_ap, f32=False):
        if not DEBUG:
            return
        arr = dbgf_d if f32 else dbg_d
        idx = len([n for n in DBG_NAMES if n[1] == f32])
        DBG_NAMES.append((name, f32, idx))
        P, Fr = src_ap.shape[0], src_ap.shape[1]
        S.op("sync", lambda g, d=arr[idx][0:P, 0:Fr], s=src_ap: g.dma_start(out=d, in_=s),
             waits=[(tv, tv.n), (vv, vv.n), (sv, sv.n)], inc=ioS, inc_k=16)

    for dst, src in ((onesc, ones_col_d), (onesr, ones_row_d), (identb, ident_bf_d),
                     (identf, ident_f_d), (maskb, mask_bf_d), (bq_c, bq_d),
                     (bk_c, bk_d), (bv_c, bv_d), (bout_c, bout_d), (lnw_c, lnw_d),
                     (rtr_t, wrtr_d), (onesb2, ones_colb_d), (ones64, ones64_d),
                     (cw_t, cw_d), (cb_t, cb_d), (dtb_t, dtb_d),
                     (alog_t, alog_d)):
        g_load(dst[...], src[...])
    for c in range(NC):
        g_load(X[c][:, :], x_in[c])
    zrow = carve(B1_OFF, 0, "zrow", [1, 2048], F32)
    S.op("vector", lambda v: v.memset(zrow[:, :], 0.0), inc=vv)
    io_store(d_rows[ROW_MUCUM:ROW_MUCUM + 1, :], zrow[:, :], waits=[vv])
    S.join()

    def emit_stats_and_center(tag, extra_rows=()):
        x2 = [carve(B1_OFF, 8192 * i, f"x2{i}_{tag}", [128, 2048], F32) for i in range(2)]
        r_mu = carve(B1_OFF, 16384, f"mu_{tag}", [1, 2048], F32)
        r_r = carve(B1_OFF, 24576, f"r_{tag}", [1, 2048], F32)
        r_t = carve(B1_OFF, 32768, f"t_{tag}", [1, 2048], F32)
        TV0 = tv.n

        for c in range(NC):
            if c % 2 == 0:
                S.op("scalar",
                     lambda s, c=c: s.activation(out=x2[c % 2][:, :],
                                                 in_=X[c][:, :],
                                                 func=AF.Square),
                     waits=[(tv, tv.n)], inc=sv)
                sq_w = (sv, sv.n)
            else:
                S.op("vector",
                     lambda v, c=c: v.tensor_tensor(x2[c % 2][:, :], X[c][:, :],
                                                    X[c][:, :], op=ALU.mult),
                     waits=[(tv, tv.n)], inc=vv)
                sq_w = (vv, vv.n)
            for nb in range(4):
                S.op("tensor", lambda t, c=c, nb=nb: t.matmul(
                    PS_B[0:1, nb * 512:(nb + 1) * 512], onesb2[:, :],
                    X[c][:, nb * 512:(nb + 1) * 512],
                    start=(c == 0), stop=(c == NC - 1)))
            for nb in range(4):
                S.op("tensor", lambda t, c=c, nb=nb: t.matmul(
                    PS_A[0:1, nb * 512:(nb + 1) * 512], onesc[:, :],
                    x2[c % 2][:, nb * 512:(nb + 1) * 512],
                    start=(c == 0), stop=(c == NC - 1)),
                    waits=([sq_w] if nb == 0 else []),
                    inc=(tv if nb == 3 else None))

        S.op("vector", lambda v: v.tensor_scalar_mul(r_mu[:, :], PS_B[0:1, :], 1.0 / H),
             waits=[tv])
        S.op("vector", lambda v: v.tensor_scalar_mul(r_t[:, :], PS_A[0:1, :], 1.0 / H))
        S.op("vector", lambda v: v.tensor_tensor(r_r[:, :], r_mu[:, :], r_mu[:, :],
                                                 op=ALU.mult))
        S.op("vector", lambda v: v.tensor_sub(r_t[:, :], r_t[:, :], r_r[:, :]))
        S.op("vector", lambda v: v.tensor_scalar_add(r_t[:, :], r_t[:, :], EPS_LN),
             inc=vv)
        S.op("scalar", lambda s: s.activation(out=r_t[:, :], in_=r_t[:, :], func=AF.Sqrt),
             waits=[vv], inc=sv)
        S.op("vector", lambda v: v.reciprocal(r_r[:, :], r_t[:, :]), waits=[sv], inc=vv)
        io_store(d_rows[ROW_R:ROW_R + 1, :], r_r[:, :], waits=[vv])
        for t16 in range(16):
            S.op("tensor", lambda t, t16=t16: t.matmul(
                PS_B[:, t16:t16 + 1], r_r[0:1, t16 * 128:(t16 + 1) * 128],
                onesr[0:1, 0:1], start=True, stop=True),
                waits=([(vv, vv.n)] if t16 == 0 else []),
                inc=(tv if t16 == 15 else None))
        S.op("vector", lambda v: v.tensor_copy(rT_t[:, :], PS_B[:, 0:16]),
             waits=[tv], inc=vv)
        if "rsc" in extra_rows:
            S.op("vector", lambda v: v.tensor_scalar_mul(rscT_t[:, :], rT_t[:, :],
                                                         1.0 / math.sqrt(H)),
                 inc=vv)
        if "r1p" in extra_rows:
            S.op("vector", lambda v: v.tensor_scalar_add(r_t[:, :], r_r[:, :], 1.0),
                 waits=[ioS], inc=vv)
            io_store(d_rows[ROW_R1P:ROW_R1P + 1, :], r_t[:, :], waits=[vv])
        if "rho3" in extra_rows:
            S.op("vector", lambda v: v.tensor_tensor(r_t[:, :], r_r[:, :], r_r[:, :],
                                                     op=ALU.mult), waits=[ioS])
            S.op("vector", lambda v: v.tensor_scalar(r_t[:, :], r_t[:, :], -EPS_LN,
                                                     1.0 + EPS_RMS, op0=ALU.mult,
                                                     op1=ALU.add), inc=vv)
            S.op("scalar", lambda s: s.activation(out=r_t[:, :], in_=r_t[:, :],
                                                  func=AF.Sqrt), waits=[vv], inc=sv)
            S.op("vector", lambda v: v.reciprocal(r_t[:, :], r_t[:, :]), waits=[sv])
            S.op("vector", lambda v: v.tensor_tensor(r_t[:, :], r_t[:, :], r_r[:, :],
                                                     op=ALU.mult), inc=vv)
            io_store(d_rows[ROW_RHO3:ROW_RHO3 + 1, :], r_t[:, :], waits=[vv])
            S.op("vector", lambda v: v.tensor_tensor(rhoT_t[:, :], rT_t[:, :],
                                                     rT_t[:, :], op=ALU.mult))
            S.op("vector", lambda v: v.tensor_scalar(rhoT_t[:, :], rhoT_t[:, :],
                                                     -EPS_LN, 1.0 + EPS_RMS,
                                                     op0=ALU.mult, op1=ALU.add),
                 inc=vv)
            S.op("scalar", lambda s: s.activation(out=rhoT_t[:, :], in_=rhoT_t[:, :],
                                                  func=AF.Sqrt), waits=[vv], inc=sv)
            S.op("vector", lambda v: v.reciprocal(rhoT_t[:, :], rhoT_t[:, :]),
                 waits=[sv])
            S.op("vector", lambda v: v.tensor_tensor(rhoT_t[:, :], rhoT_t[:, :],
                                                     rT_t[:, :], op=ALU.mult),
                 inc=vv)
        r_mc = carve(B1_OFF, 40960, f"mc_{tag}", [1, 2048], F32)
        io_load(r_mc[:, :], d_rows[ROW_MUCUM:ROW_MUCUM + 1, :], parity=1, waits=[ioS])
        S.op("vector", lambda v: v.tensor_add(r_mc[:, :], r_mc[:, :], r_mu[:, :]),
             waits=[io[1]], inc=vv)
        io_store(d_rows[ROW_MUCUM:ROW_MUCUM + 1, :], r_mc[:, :], waits=[vv])
        for nb in range(4):
            S.op("tensor", lambda t, nb=nb: t.matmul(
                PS_A[:, nb * 512:(nb + 1) * 512], onesr[:, :],
                r_mu[0:1, nb * 512:(nb + 1) * 512], start=True, stop=True),
                waits=[vv], inc=(tv if nb == 3 else None))
        for c in range(NC):
            S.op("vector", lambda v, c=c: v.tensor_tensor(
                X[c][:, :], X[c][:, :], PS_A[:, :], op=ALU.subtract),
                 waits=[tv], inc=(vv if c == NC - 1 else None))
        S.join()

    def emit_row_bcast(row_sb, dst_f32, ps, scale_const=None, extra_waits=()):
        for nb in range(4):
            S.op("tensor", lambda t, nb=nb: t.matmul(
                ps[:, nb * 512:(nb + 1) * 512], onesr[:, :],
                row_sb[0:1, nb * 512:(nb + 1) * 512], start=True, stop=True),
                waits=list(extra_waits), inc=(tv if nb == 3 else None))
        if scale_const is None:
            S.op("vector", lambda v: v.tensor_copy(dst_f32[:, :], ps[:, :]),
                 waits=[tv], inc=vv)
        else:
            S.op("vector", lambda v: v.tensor_scalar_mul(dst_f32[:, :], ps[:, :],
                                                         scale_const),
                 waits=[tv], inc=vv)

    def emit_attn():
        emit_stats_and_center("a1", extra_rows=("rsc",))
        tap("xc0", X[0][:, :])
        tap("xc15", X[15][:, :])

        wvh = [carve(B1_OFF, 2048 * c, f"wvh{c}", [128, 1024], BF16) for c in range(NC)]
        for oh in range(2):
            for c in range(NC):
                g_load(wvh[c][:, :], wv_d[oh, c], ctr=gW[oh],
                       waits=[(tv, tv.n)])
            avm = [sv.n, sv.n]
            for tt in range(16):
                ps = PS_A if tt % 2 == 0 else PS_B
                for c in range(NC):
                    for nb in range(2):
                        S.op("tensor", lambda t, c=c, nb=nb, tt=tt, ps=ps: t.matmul(
                            ps[:, nb * 512:(nb + 1) * 512],
                            X[c][:, tt * 128:(tt + 1) * 128],
                            wvh[c][:, nb * 512:(nb + 1) * 512],
                            start=(c == 0), stop=(c == NC - 1)),
                            waits=[(gW[oh], gW[oh].n), (sv, avm[tt % 2])],
                            inc=(tv if (c == NC - 1 and nb == 1) else None))
                ob = OBUF[tt % 2]
                S.op("scalar", lambda s, ps=ps, ob=ob: s.activation(
                    out=ob[:, :], in_=ps[:, 0:1024], func=AF.Copy),
                    waits=[tv, (ioT[tt % 2], ioT[tt % 2].n)], inc=sv)
                io_store(d_v[tt][:, oh * 1024:(oh + 1) * 1024], ob[:, :],
                         waits=[sv], parity=tt % 2)
                avm[tt % 2] = sv.n
        S.join()

        qT = [carve(B1_OFF, 4096 * i, f"qT{i}", [128, 2048], BF16) for i in range(16)]
        kT = [carve(B2_OFF, 4096 * i, f"kT{i}", [128, 2048], BF16) for i in range(16)]
        marks = [tv.n, tv.n]
        jobs = [(wq_d, qT, bq_c, ot) for ot in range(16)] + \
               [(wk_d, kT, bk_c, ot) for ot in range(16)]
        qkm = [sv.n, sv.n]
        for j, (wsrc, dstT, bcol, ot) in enumerate(jobs):
            par = j % 2
            g_load(WBUF[par][:, :], wsrc[ot], ctr=gW[par], waits=[(tv, marks[par])])
            ps = PS_A if par == 0 else PS_B
            for c in range(NC):
                for nb in range(4):
                    S.op("tensor", lambda t, c=c, nb=nb, par=par, ps=ps: t.matmul(
                        ps[:, nb * 512:(nb + 1) * 512],
                        WBUF[par][:, c * 128:(c + 1) * 128],
                        X[c][:, nb * 512:(nb + 1) * 512],
                        start=(c == 0), stop=(c == NC - 1)),
                        waits=[(gW[par], gW[par].n), (sv, qkm[par])],
                        inc=(tv if (c == NC - 1 and nb == 3) else None))
            marks[par] = tv.n
            S.op("scalar", lambda s, ps=ps, d=dstT[ot], b=bcol, ot=ot: s.activation(
                out=d[:, :], in_=ps[:, :], func=AF.Identity, bias=b[:, ot:ot + 1]),
                waits=[tv], inc=sv)
            qkm[par] = sv.n
        S.join()
        tap("q0", qT[0][:, :])
        tap("k0", kT[0][:, :])

        for c in range(NC):
            io_store(d_x[c], X[c][:, :])
        XSPILL = ioS.n
        S.join()

        scores = carve(X_OFF, 0, "scoresB", [128, 2048], F32)
        probs = carve(X_OFF, 8192, "probsB", [128, 2048], BF16)
        probs2 = carve(X_OFF, 12288, "probs2B", [128, 2048], BF16)
        probsT = carve(X_OFF, 16384, "probsTB", [128, 2048], BF16)
        R_b = carve(X_OFF, 20480, "R_bB", [128, 2048], F32)
        rrow = carve(X_OFF, 28672, "rrowB", [1, 2048], F32)
        io_load(rrow[:, :], d_rows[ROW_R:ROW_R + 1, :], parity=0, waits=[(ioS, XSPILL)])
        emit_row_bcast(rrow, R_b, PS_A, extra_waits=[(io[0], io[0].n)])
        S.join()

        bmark = [vv.n, vv.n]

        def emit_scores_mms(qt):
            ps = PS_A if qt % 2 == 0 else PS_B
            for c in range(NC):
                for nb in range(4):
                    S.op("tensor", lambda t, c=c, nb=nb, qt=qt, ps=ps: t.matmul(
                        ps[:, nb * 512:(nb + 1) * 512],
                        qT[c][:, qt * 128:(qt + 1) * 128],
                        kT[c][:, nb * 512:(nb + 1) * 512],
                        start=(c == 0), stop=(c == NC - 1)),
                        waits=([(vv, bmark[qt % 2])]
                               if (c == 0 and nb == 0) else []),
                        inc=(tv if (c == NC - 1 and nb == 3) else None))

        emit_scores_mms(0)
        for qt in range(16):
            ps = PS_A if qt % 2 == 0 else PS_B
            psbf = PS_Abf if qt % 2 == 0 else PS_Bbf
            SCM = tv.n
            S.op("vector", lambda v, ps=ps, qt=qt: v.scalar_tensor_tensor(
                scores[:, :], ps[:, :], rscT_t[:, qt:qt + 1], R_b[:, :],
                op0=ALU.mult, op1=ALU.mult),
                waits=[(tv, SCM), (sv, sv.n)], inc=vv)
            if qt < 15:
                emit_scores_mms(qt + 1)
            if qt == 0:
                tap("scores0", scores[:, :], f32=True)
                if DEBUG:
                    S.join()
            S.op("vector", lambda v: v.reduce_max(nmax_t[:, 1:2], scores[:, :],
                                                  axis=AX.X, negate=True), inc=vv)
            S.op("scalar", lambda s: s.activation(
                out=probs[:, :], in_=scores[:, :], func=AF.Exp,
                bias=nmax_t[:, 1:2], accum_out=nmax_t[:, 2:3]), waits=[vv], inc=sv)
            S.op("vector", lambda v: v.reciprocal(nmax_t[:, 3:4], nmax_t[:, 2:3]),
                 waits=[sv], inc=vv)
            S.op("vector", lambda v: v.scalar_tensor_tensor(
                probs2[:, :], probs[:, :], nmax_t[:, 3:4], R_b[:, :],
                op0=ALU.mult, op1=ALU.mult), inc=vv)
            if qt == 0:
                tap("probs0", probs[:, :])
                tap("probs20", probs2[:, :])
                tap("nmax0", nmax_t[:, :], f32=True)
                if DEBUG:
                    S.join()
            for kc in range(16):
                S.op("tensor", lambda t, kc=kc, psbf=psbf: t.transpose(
                    psbf[:, kc * 128:(kc + 1) * 128],
                    probs2[:, kc * 128:(kc + 1) * 128], identb[:, :]),
                    waits=[(vv, vv.n)], inc=(tv if kc == 15 else None))
            S.op("vector", lambda v, psbf=psbf: v.tensor_copy(
                probsT[:, :], psbf[:, 0:2048]),
                waits=[tv, (ioT[0], ioT[0].n), (ioT[1], ioT[1].n)], inc=vv)
            bmark[qt % 2] = vv.n
            if qt == 0:
                tap("probsT0", probsT[:, :])
                if DEBUG:
                    S.join()
            io_store(d_pT[qt], probsT[:, :], waits=[vv], parity=qt % 2)
        S.join()

        vT = [carve(B2_OFF, 4096 * i, f"vT{i}", [128, 2048], BF16) for i in range(16)]
        for i in range(16):
            io_load(vT[i][:, :], d_v[i], parity=i % 2)
        ctx = [carve(B1_OFF, 4096 * i, f"ctx{i}", [128, 2048], BF16) for i in range(16)]
        pt4 = [carve(X_OFF, 0, "pt4_0", [128, 8192], BF16),
               carve(X_OFF, 16384, "pt4_1", [128, 8192], BF16)]
        VL = (io[0].n, io[1].n)
        cmark = [tv.n, tv.n]
        smarkC = [sv.n, sv.n]
        PTL = (io[0].n, io[1].n)
        for qg in range(4):
            gpar = qg % 2
            buf = pt4[gpar]
            b3 = buf[:, :].rearrange("p (k j n) -> p k j n", j=4, n=128)
            for j in range(4):
                io_load(b3[:, :, j, :], d_pT[qg * 4 + j], parity=j % 2,
                        waits=[(tv, cmark[gpar]), (sv, smarkC[1 - gpar]),
                               (io[0], PTL[0]), (io[1], PTL[1])])
            PTL = (io[0].n, io[1].n)
            for htg in range(4):
                ps = PS_A if htg % 2 == 0 else PS_B
                for htl in range(4):
                    ht = htg * 4 + htl
                    for kc in range(16):
                        S.op("tensor", lambda t, ht=ht, htl=htl, kc=kc, ps=ps,
                             buf=buf: t.matmul(
                            ps[:, htl * 512:(htl + 1) * 512],
                            vT[kc][:, ht * 128:(ht + 1) * 128],
                            buf[:, kc * 512:(kc + 1) * 512],
                            start=(kc == 0), stop=(kc == 15)),
                            waits=[(io[0], PTL[0]), (io[1], PTL[1]),
                                   (sv, sv.n)],
                            inc=(tv if kc == 15 else None))
                for htl in range(4):
                    ht = htg * 4 + htl
                    S.op("scalar", lambda s, ht=ht, htl=htl, qg=qg, ps=ps: s.activation(
                        out=ctx[ht][:, qg * 512:(qg + 1) * 512],
                        in_=ps[:, htl * 512:(htl + 1) * 512],
                        func=AF.Identity, bias=bv_c[:, ht:ht + 1]),
                        waits=[(tv, tv.n)], inc=sv)
            cmark[gpar] = tv.n
        S.join()
        tap("ctx0", ctx[0][:, :])
        tap("ctx15", ctx[15][:, :])

        marks = [tv.n, tv.n]
        vmarks = [vv.n, vv.n]
        for ot in range(16):
            par = ot % 2
            g_load(WBUF[par][:, :], wout_d[ot], ctr=gW[par], waits=[(tv, marks[par])])
            io_load(X[ot][:, :], d_x[ot], parity=par,
                    waits=[(tv, marks[par]), (vv, vmarks[par])])
            ps = PS_A if par == 0 else PS_B
            for hc in range(NC):
                for nb in range(4):
                    S.op("tensor", lambda t, hc=hc, nb=nb, par=par, ps=ps: t.matmul(
                        ps[:, nb * 512:(nb + 1) * 512],
                        WBUF[par][:, hc * 128:(hc + 1) * 128],
                        ctx[hc][:, nb * 512:(nb + 1) * 512],
                        start=(hc == 0), stop=(hc == NC - 1)),
                        waits=[(gW[par], gW[par].n), (vv, vmarks[par])],
                        inc=(tv if (hc == NC - 1 and nb == 3) else None))
            marks[par] = tv.n
            S.op("vector", lambda v, ot=ot, ps=ps, par=par: v.scalar_tensor_tensor(
                X[ot][:, :], ps[:, :], bout_c[:, ot:ot + 1], X[ot][:, :],
                op0=ALU.add, op1=ALU.add),
                waits=[tv, (io[par], io[par].n)], inc=vv)
            vmarks[par] = vv.n
        S.join()

    def emit_final():
        mcrow = carve(B1_OFF, 0, "mcrow_f", [1, 2048], F32)
        ofin = [carve(B1_OFF, 8192 + 8192 * i, f"ofin{i}", [128, 2048], F32)
                for i in range(2)]
        io_load(mcrow[:, :], d_rows[ROW_MUCUM:ROW_MUCUM + 1, :], parity=0,
                waits=[(ioS, ioS.n)])
        for nb in range(4):
            S.op("tensor", lambda t, nb=nb: t.matmul(
                PS_B[:, nb * 512:(nb + 1) * 512], onesr[:, :],
                mcrow[0:1, nb * 512:(nb + 1) * 512], start=True, stop=True),
                waits=[(io[0], io[0].n)], inc=(tv if nb == 3 else None))
        for c in range(NC):
            S.op("vector", lambda v, c=c: v.tensor_tensor(
                ofin[c % 2][:, :], X[c][:, :], PS_B[:, :], op=ALU.add),
                waits=[tv, (ioT[c % 2], ioT[c % 2].n)], inc=vv)
            io_store(out_d[c], ofin[c % 2][:, :], waits=[vv], parity=c % 2)
        S.op("sync", lambda g: g.nop(),
             waits=[(ioT[0], ioT[0].n), (ioT[1], ioT[1].n), (ioS, ioS.n)])

    moe_mamba = _make_moe_mamba(
        nc=nc, S=S, carve=carve, X=X, B1_OFF=B1_OFF, B2_OFF=B2_OFF, X_OFF=X_OFF,
        PS_A=PS_A, PS_B=PS_B, PS_Abf=PS_Abf, PS_Bbf=PS_Bbf,
        WBUF=WBUF, OBUF=OBUF, gW=gW, gM=gM, io=io, ioT=ioT, ioS=ioS, gA=gA, gG=gG,
        tv=tv, vv=vv, sv=sv, pv=pv, g_load=g_load, io_load=io_load, io_store=io_store,
        emit_stats_and_center=emit_stats_and_center, emit_row_bcast=emit_row_bcast,
        tap=tap,
        d=dict(d_x=d_x, d_v=d_v, d_pT=d_pT, d_gate=d_gate, d_raw=d_raw,
               d_rows=d_rows, d_flat=d_flat, d_gfm=d_gfm, d_om=d_om),
        w=dict(w1_d=w1_d, w3_d=w3_d, w2_d=w2_d, wg_d=wg_d, wxs_d=wxs_d,
               wbc_d=wbc_d, wdt_d=wdt_d, drow_d=drow_d, nw_d=nw_d, wop_d=wop_d),
        t=dict(onesc=onesc, onesr=onesr, ones64=ones64, identb=identb,
               identf=identf, maskb=maskb,
               lnw_c=lnw_c, ilnw_c=ilnw_c, rT_t=rT_t, rscT_t=rscT_t, rhoT_t=rhoT_t,
               rtr_t=rtr_t, cw_t=cw_t, cb_t=cb_t, dtb_t=dtb_t, alog_t=alog_t,
               a_t=a_t, nmax_t=nmax_t),
        rows=dict(ROW_MUCUM=ROW_MUCUM, ROW_R=ROW_R, ROW_RHO3=ROW_RHO3,
                  ROW_R1P=ROW_R1P, ROW_RSC=ROW_RSC))

    if DO_ATTN:
        emit_attn()
    if DO_MOE1:
        moe_mamba.emit_moe("m1")
    if DO_MAMBA:
        moe_mamba.emit_mamba()
    if DO_MOE2:
        moe_mamba.emit_moe("m2")
    emit_final()

    with nc.Block() as block:
        S.emit(block)
    return nc


class _make_moe_mamba:
    def __init__(self, **kw):
        self.__dict__.update(kw)

    def emit_moe(self, tag):
        S, carve, X = self.S, self.carve, self.X
        B1_OFF, B2_OFF = self.B1_OFF, self.B2_OFF
        PS_A, PS_B = self.PS_A, self.PS_B
        WBUF = self.WBUF
        gW, io, ioS = self.gW, self.io, self.ioS
        tv, vv, sv = self.tv, self.vv, self.sv
        t, d, w, rows = self.t, self.d, self.w, self.rows
        g_load, io_load, io_store = self.g_load, self.io_load, self.io_store
        onesr, identf = t["onesr"], t["identf"]

        self.emit_stats_and_center(tag)

        xn = [carve(B1_OFF, 4096 * i, f"xn{i}_{tag}", [128, 2048], BF16)
              for i in range(16)]
        R_b = carve(B2_OFF, 0, f"Rb_{tag}", [128, 2048], F32)
        Gb = carve(B2_OFF, 8192, f"Gb_{tag}", [128, 2048], F32)
        hsil = carve(B2_OFF, 16384, f"hsil_{tag}", [128, 2048], BF16)
        hpr = [carve(B2_OFF, 20480 + 4096 * i, f"hpr{i}_{tag}", [128, 2048], BF16)
               for i in range(4)]
        gfm = carve(B2_OFF, 36864, f"gfm_{tag}", [8, 2048], F32)
        grow = carve(B2_OFF, 45056, f"grow_{tag}", [1, 2048], F32)
        sct = carve(B2_OFF, 53248, f"sct_{tag}", [128, 128], F32)
        u = [carve(B2_OFF, 53760 + 512 * i, f"u{i}_{tag}", [128, 128], F32)
             for i in range(6)]
        m12 = carve(B2_OFF, 56832, f"m12_{tag}", [128, 64], F32)
        rrow = carve(B2_OFF, 57344, f"rrow_{tag}", [1, 2048], F32)

        self.tap(f"xcm_{tag}", X[0][:, :])
        io_load(rrow[:, :], d["d_rows"][rows["ROW_R"]:rows["ROW_R"] + 1, :],
                parity=0, waits=[ioS])
        self.emit_row_bcast(rrow, R_b, PS_A, extra_waits=[(io[0], io[0].n)])
        for c in range(16):
            S.op("vector", lambda v, c=c: v.tensor_tensor(
                xn[c][:, :], X[c][:, :], R_b[:, :], op=ALU.mult),
                waits=[vv], inc=(vv if c == 15 else None))
        self.tap(f"xn0_{tag}", xn[0][:, :])

        for tt in range(16):
            for fc in range(16):
                S.op("tensor", lambda tn, tt=tt, fc=fc: tn.matmul(
                    PS_A[:, tt * 8:(tt + 1) * 8],
                    xn[fc][:, tt * 128:(tt + 1) * 128],
                    self.t["rtr_t"][:, fc * 8:(fc + 1) * 8],
                    start=(fc == 0), stop=(fc == 15)),
                    waits=[(vv, vv.n)],
                    inc=(tv if (tt == 15 and fc == 15) else None))
        S.op("vector", lambda v: v.tensor_copy(sct[:, :], PS_A[:, 0:128]),
             waits=[tv], inc=vv)
        self.tap(f"sct_{tag}", sct[:, :], f32=True)
        if DEBUG:
            S.join()
        BIG = 1e8
        sct3 = sct[:, :].rearrange("p (t e) -> p t e", e=8)
        def m1s(v):
            for tt in range(16):
                v.reduce_max(m12[:, tt:tt + 1], sct[:, tt * 8:(tt + 1) * 8], axis=AX.X)
            return v.engine_nop()
        S.op("vector", m1s, inc=vv)
        S.op("vector", lambda v: v.tensor_tensor(
            u[0][:, :].rearrange("p (t e) -> p t e", e=8), sct3,
            m12[:, 0:16].unsqueeze(2).broadcast_to([128, 16, 8]),
            op=ALU.subtract), inc=vv)
        S.op("scalar", lambda s: s.activation(out=u[1][:, :], in_=u[0][:, :],
                                              func=AF.Exp, scale=BIG),
             waits=[vv], inc=sv)
        S.op("vector", lambda v: v.scalar_tensor_tensor(
            u[2][:, :], u[1][:, :], -1e30, sct[:, :], op0=ALU.mult, op1=ALU.add),
            waits=[sv], inc=vv)
        def m2s(v):
            for tt in range(16):
                v.reduce_max(m12[:, 16 + tt:17 + tt], u[2][:, tt * 8:(tt + 1) * 8],
                             axis=AX.X)
            return v.engine_nop()
        S.op("vector", m2s, inc=vv)
        S.op("vector", lambda v: v.tensor_tensor(
            u[3][:, :].rearrange("p (t e) -> p t e", e=8), sct3,
            m12[:, 16:32].unsqueeze(2).broadcast_to([128, 16, 8]),
            op=ALU.subtract))
        S.op("vector", lambda v: v.tensor_scalar_min(u[3][:, :], u[3][:, :], 0.0),
             inc=vv)
        S.op("scalar", lambda s: s.activation(out=u[4][:, :], in_=u[3][:, :],
                                              func=AF.Exp, scale=BIG),
             waits=[vv], inc=sv)
        S.op("vector", lambda v: v.tensor_scalar(u[5][:, :], u[1][:, :], -1.0, 1.0,
                                                 op0=ALU.mult, op1=ALU.add),
             waits=[sv])
        S.op("vector", lambda v: v.tensor_tensor(u[4][:, :], u[4][:, :], u[5][:, :],
                                                 op=ALU.mult))
        S.op("vector", lambda v: v.tensor_tensor(m12[:, 32:48], m12[:, 0:16],
                                                 m12[:, 16:32], op=ALU.subtract),
             inc=vv)
        S.op("scalar", lambda s: s.activation(out=m12[:, 48:64], in_=m12[:, 32:48],
                                              func=AF.Sigmoid), waits=[vv], inc=sv)
        S.op("vector", lambda v: v.tensor_scalar(m12[:, 32:48], m12[:, 48:64],
                                                 -1.0, 1.0, op0=ALU.mult,
                                                 op1=ALU.add), waits=[sv])
        S.op("vector", lambda v: v.tensor_tensor(
            u[1][:, :].rearrange("p (t e) -> p t e", e=8),
            u[1][:, :].rearrange("p (t e) -> p t e", e=8),
            m12[:, 48:64].unsqueeze(2).broadcast_to([128, 16, 8]), op=ALU.mult))
        S.op("vector", lambda v: v.tensor_tensor(
            u[4][:, :].rearrange("p (t e) -> p t e", e=8),
            u[4][:, :].rearrange("p (t e) -> p t e", e=8),
            m12[:, 32:48].unsqueeze(2).broadcast_to([128, 16, 8]), op=ALU.mult))
        S.op("vector", lambda v: v.tensor_tensor(u[0][:, :], u[1][:, :], u[4][:, :],
                                                 op=ALU.add), inc=vv)
        self.tap(f"gt_{tag}", u[0][:, :], f32=True)
        self.tap(f"m12_{tag}", m12[:, :], f32=True)
        if DEBUG:
            S.join()
        for tt in range(16):
            S.op("tensor", lambda tn, tt=tt: tn.transpose(
                PS_B[0:8, tt * 128:(tt + 1) * 128], u[0][:, tt * 8:(tt + 1) * 8],
                identf[:, :]), waits=[(vv, vv.n)],
                inc=(tv if tt == 15 else None))
        S.op("vector", lambda v: v.tensor_copy(gfm[:, :], PS_B[0:8, :]),
             waits=[tv], inc=vv)
        self.tap(f"gfm_{tag}", gfm[:, :], f32=True)
        io_store(d["d_gfm"][:, :], gfm[:, :], waits=[vv])
        GSTORED = ioS.n

        self.hsilm = vv.n
        self.hprm = vv.n
        marks = [tv.n, tv.n]
        wj = 0
        gbm = tv.n
        for e in range(8):
            io_load(grow[:, :], d["d_gfm"][e:e + 1, :], parity=0,
                    waits=[(ioS, GSTORED), (tv, gbm),
                           (io[0], io[0].n), (io[1], io[1].n)])
            self.emit_row_bcast(grow, Gb, PS_B,
                                extra_waits=[(io[0], io[0].n), (vv, vv.n)])
            gbm = tv.n
            self.hsilm = max(self.hsilm, vv.n)
            for ot in range(4):
                par = wj % 2; wj += 1
                g_load(WBUF[par][:, :], w["w1_d"][e, ot], ctr=gW[par],
                       waits=[(tv, marks[par])])
                for fc in range(16):
                    for nb in range(4):
                        S.op("tensor", lambda tn, fc=fc, nb=nb, par=par: tn.matmul(
                            PS_A[:, nb * 512:(nb + 1) * 512],
                            WBUF[par][:, fc * 128:(fc + 1) * 128],
                            xn[fc][:, nb * 512:(nb + 1) * 512],
                            start=(fc == 0), stop=(fc == 15)),
                            waits=([(gW[par], gW[par].n), (sv, sv.n)]
                                   if (fc == 0 and nb == 0) else []),
                            inc=(tv if (fc == 15 and nb == 3) else None))
                marks[par] = tv.n
                S.op("scalar", lambda s: s.activation(out=hsil[:, :], in_=PS_A[:, :],
                                                      func=AF.Silu),
                     waits=[tv, (vv, self.hprm)], inc=sv)
                par = wj % 2; wj += 1
                g_load(WBUF[par][:, :], w["w3_d"][e, ot], ctr=gW[par],
                       waits=[(tv, marks[par])])
                for fc in range(16):
                    for nb in range(4):
                        S.op("tensor", lambda tn, fc=fc, nb=nb, par=par: tn.matmul(
                            PS_B[:, nb * 512:(nb + 1) * 512],
                            WBUF[par][:, fc * 128:(fc + 1) * 128],
                            xn[fc][:, nb * 512:(nb + 1) * 512],
                            start=(fc == 0), stop=(fc == 15)),
                            waits=([(gW[par], gW[par].n), (sv, sv.n),
                                    (vv, self.hsilm)]
                                   if (fc == 0 and nb == 0) else []),
                            inc=(tv if (fc == 15 and nb == 3) else None))
                marks[par] = tv.n
                S.op("vector", lambda v: v.tensor_tensor(hsil[:, :], hsil[:, :],
                                                         PS_B[:, :], op=ALU.mult),
                     waits=[tv, sv])
                self.hsilm = vv.n
                S.op("vector", lambda v, ot=ot: v.tensor_tensor(
                    hpr[ot][:, :], hsil[:, :], Gb[:, :], op=ALU.mult), inc=vv)
                self.hprm = vv.n
                if e == 0 and ot == 0:
                    self.tap(f"hpr0_{tag}", hpr[0][:, :])
                    self.tap(f"Gb0_{tag}", Gb[:, :], f32=True)
                    if DEBUG:
                        S.join()
            w2m = [vv.n, vv.n]
            for dt16 in range(16):
                par = wj % 2; wj += 1
                g_load(WBUF[par][:, 0:512], w["w2_d"][e, dt16], ctr=gW[par],
                       waits=[(tv, marks[par])])
                ps = PS_A if dt16 % 2 == 0 else PS_B
                for fc in range(4):
                    for nb in range(4):
                        S.op("tensor", lambda tn, fc=fc, nb=nb, par=par, ps=ps: tn.matmul(
                            ps[:, nb * 512:(nb + 1) * 512],
                            WBUF[par][:, fc * 128:(fc + 1) * 128],
                            hpr[fc][:, nb * 512:(nb + 1) * 512],
                            start=(fc == 0), stop=(fc == 3)),
                            waits=([(gW[par], gW[par].n), (vv, w2m[dt16 % 2]),
                                    (sv, sv.n)]
                                   if (fc == 0 and nb == 0) else []),
                            inc=(tv if (fc == 3 and nb == 3) else None))
                marks[par] = tv.n
                S.op("vector", lambda v, dt16=dt16, ps=ps: v.tensor_tensor(
                    X[dt16][:, :], X[dt16][:, :], ps[:, :], op=ALU.add),
                    waits=[tv], inc=vv)
                w2m[dt16 % 2] = vv.n
            if e == 0:
                self.tap(f"xe0_{tag}", X[0][:, :])
                if DEBUG:
                    S.join()
        S.join()
        self.tap(f"xout_{tag}", X[0][:, :])

    def emit_mamba(self):
        S, carve, X = self.S, self.carve, self.X
        B1_OFF, B2_OFF, X_OFF = self.B1_OFF, self.B2_OFF, self.X_OFF
        PS_A, PS_B = self.PS_A, self.PS_B
        PS_Abf, PS_Bbf = self.PS_Abf, self.PS_Bbf
        WBUF, OBUF = self.WBUF, self.OBUF
        gW, io, ioT, ioS = self.gW, self.io, self.ioT, self.ioS
        gA = self.gA
        gG = self.gG
        tv, vv, sv = self.tv, self.vv, self.sv
        pv = self.pv
        t, d, w, rows = self.t, self.d, self.w, self.rows
        g_load, io_load, io_store = self.g_load, self.io_load, self.io_store
        onesr, ones64 = t["onesr"], t["ones64"]
        identb, identf = t["identb"], t["identf"]
        maskb = t["maskb"]
        lnw_c, rhoT_t = t["lnw_c"], t["rhoT_t"]
        cw_t, cb_t, dtb_t, alog_t, a_t = (t["cw_t"], t["cb_t"], t["dtb_t"],
                                          t["alog_t"], t["a_t"])
        tap = self.tap

        self.emit_stats_and_center("mb", extra_rows=("rho3", "r1p"))

        XL = [carve(B2_OFF, 4096 * i, f"XL{i}", [128, 2048], BF16) for i in range(16)]
        wgh = [carve(B1_OFF, 4096 * i, f"wgh{i}", [128, 2048], BF16) for i in range(16)]
        for c in range(16):
            S.op("vector", lambda v, c=c: v.tensor_scalar_mul(
                XL[c][:, :], X[c][:, :], lnw_c[:, c:c + 1]),
                inc=(vv if c == 15 else None))
        tap("XL0", XL[0][:, :])
        for oh in range(2):
            for fc in range(16):
                g_load(wgh[fc][:, :], w["wg_d"][oh, fc], ctr=gW[oh],
                       waits=[(tv, tv.n)])
            g1m = [sv.n, sv.n]
            for tt in range(16):
                ps = PS_A if tt % 2 == 0 else PS_B
                for fc in range(16):
                    for nb in range(4):
                        S.op("tensor", lambda tn, fc=fc, nb=nb, tt=tt, ps=ps: tn.matmul(
                            ps[:, nb * 512:(nb + 1) * 512],
                            XL[fc][:, tt * 128:(tt + 1) * 128],
                            wgh[fc][:, nb * 512:(nb + 1) * 512],
                            start=(fc == 0), stop=(fc == 15)),
                            waits=[(gW[oh], gW[oh].n), (vv, vv.n),
                                   (sv, g1m[tt % 2])],
                            inc=(tv if (fc == 15 and nb == 3) else None))
                for half in range(2):
                    ob = OBUF[half]
                    S.op("scalar", lambda s, ps=ps, ob=ob, half=half, tt=tt: s.activation(
                        out=ob[:, :], in_=ps[:, half * 1024:(half + 1) * 1024],
                        func=AF.Silu, scale=rhoT_t[:, tt:tt + 1]),
                        waits=[tv, (ioT[half], ioT[half].n)], inc=sv)
                    io_store(d["d_gate"][tt, oh * 2 + half], ob[:, :],
                             waits=[sv], parity=half)
                g1m[tt % 2] = sv.n
        S.join()

        sA = carve(B1_OFF, 0, "m2_sA", [64, 2048], F32)
        sB = carve(B1_OFF, 8192, "m2_sB", [64, 2048], F32)
        dt_f = carve(B1_OFF, 16384, "m2_dtf", [64, 2048], F32)
        RHO3_b = carve(B1_OFF, 24576, "RHO3b", [128, 2048], F32)
        wdt_t = [carve(B1_OFF, 32768 + 128 * i, f"wdt{i}", [128, 64], BF16)
                 for i in range(16)]
        RAW_BYTES = 34848
        dtT = carve(B1_OFF, RAW_BYTES, "dtT", [128, 1024], F32)
        acumT = carve(B1_OFF, RAW_BYTES + 4096, "acumT", [128, 1024], F32)
        alast_c = carve(B1_OFF, RAW_BYTES + 8192, "alast_c", [64, 16], F32)
        NW_b = carve(B1_OFF, RAW_BYTES + 8288, "NW_b", [128, 4096], BF16)
        aflat = carve(B1_OFF, RAW_BYTES + 16480, "aflat", [1, 2048], F32)
        rho_row = carve(B1_OFF, 16384, "rho_row", [1, 2048], F32)

        io_load(rho_row[:, :], d["d_rows"][rows["ROW_RHO3"]:rows["ROW_RHO3"] + 1, :],
                parity=0, waits=[ioS])
        self.emit_row_bcast(rho_row, RHO3_b, PS_A,
                            extra_waits=[(io[0], io[0].n), (vv, vv.n), (sv, sv.n)])
        for i in range(16):
            g_load(wdt_t[i][:, :], w["wdt_d"][i], ctr=gW[0], waits=[(tv, tv.n)])
        for fc in range(16):
            for nb in range(4):
                S.op("tensor", lambda tn, fc=fc, nb=nb: tn.matmul(
                    PS_B[0:64, nb * 512:(nb + 1) * 512], wdt_t[fc][:, :],
                    XL[fc][:, nb * 512:(nb + 1) * 512],
                    start=(fc == 0), stop=(fc == 15)),
                    waits=[(gW[0], gW[0].n), (vv, vv.n)],
                    inc=(tv if (fc == 15 and nb == 3) else None))
        S.op("vector", lambda v: v.tensor_tensor(sA[:, :], PS_B[0:64, :],
                                                 RHO3_b[0:64, :], op=ALU.mult),
             waits=[tv])
        S.op("scalar", lambda s: s.activation(out=sA[:, :], in_=sA[:, :],
                                              func=AF.Identity,
                                              bias=dtb_t[:, 0:1]),
             waits=[(vv, vv.n)], inc=sv)
        S.op("scalar", lambda s: s.activation(out=sB[:, :], in_=sA[:, :], func=AF.Abs))
        S.op("scalar", lambda s: s.activation(out=sB[:, :], in_=sB[:, :], func=AF.Exp,
                                              scale=-1.0))
        S.op("scalar", lambda s: s.activation(out=sB[:, :], in_=sB[:, :], func=AF.Ln,
                                              bias=1.0), inc=sv)
        S.op("vector", lambda v: v.tensor_scalar_max(dt_f[:, :], sA[:, :], 0.0),
             waits=[sv])
        S.op("vector", lambda v: v.tensor_add(dt_f[:, :], dt_f[:, :], sB[:, :]),
             inc=vv)
        tap("dtf", dt_f[0:64, :], f32=True)
        S.op("scalar", lambda s: s.activation(out=a_t[:, :], in_=alog_t[:, :],
                                              func=AF.Exp), inc=sv)
        S.op("vector", lambda v: v.tensor_scalar_mul(a_t[:, :], a_t[:, :], -1.0),
             waits=[sv])
        S.op("vector", lambda v: v.tensor_scalar_mul(sA[:, :], dt_f[:, :],
                                                     a_t[:, 0:1]))
        z128 = carve(B1_OFF, 32768 + 2048 + 32, "z128", [64, 128], F32)
        S.op("vector", lambda v: v.memset(z128[:, :], 0.0))
        for c in range(16):
            sl = slice(c * 128, (c + 1) * 128)
            S.op("vector", lambda v, sl=sl: v.tensor_tensor_scan(
                sB[:, sl], sA[:, sl], z128[:, :], 0.0, op0=ALU.add, op1=ALU.add))
        ACUM = sB
        tap("acum", ACUM[0:64, :], f32=True)
        for c in range(16):
            S.op("vector", lambda v, c=c: v.tensor_copy(
                alast_c[:, c:c + 1], ACUM[:, c * 128 + 127:c * 128 + 128]),
                inc=(vv if c == 15 else None))
        for c in range(16):
            S.op("tensor", lambda tn, c=c: tn.transpose(
                PS_A[:, 0:64], dt_f[:, c * 128:(c + 1) * 128], identf[0:64, 0:64]),
                waits=[(vv, vv.n)], inc=tv)
            S.op("vector", lambda v, c=c: v.tensor_copy(
                dtT[:, c * 64:(c + 1) * 64], PS_A[:, 0:64]), waits=[tv], inc=vv)
            S.op("tensor", lambda tn, c=c: tn.transpose(
                PS_A[:, 64:128], ACUM[:, c * 128:(c + 1) * 128],
                identf[0:64, 0:64]), waits=[(vv, vv.n)], inc=tv)
            S.op("vector", lambda v, c=c: v.tensor_copy(
                acumT[:, c * 64:(c + 1) * 64], PS_A[:, 64:128]), waits=[tv], inc=vv)
        for c in range(16):
            io_store(d["d_flat"][c], ACUM[:, c * 128:(c + 1) * 128],
                     waits=[(vv, vv.n)])
        nwrow = carve(B1_OFF, 0, "nwrow", [1, 4096], F32)
        io_load(nwrow[:, :], w["nw_d"][:, :], parity=0,
                waits=[(vv, vv.n), (ioS, ioS.n)])
        for h0 in range(2):
            for nb in range(4):
                S.op("tensor", lambda tn, h0=h0, nb=nb: tn.matmul(
                    PS_B[:, nb * 512:(nb + 1) * 512], onesr[:, :],
                    nwrow[0:1, h0 * 2048 + nb * 512:h0 * 2048 + (nb + 1) * 512],
                    start=True, stop=True),
                    waits=[(io[0], io[0].n), (vv, vv.n)],
                    inc=(tv if nb == 3 else None))
            S.op("vector", lambda v, h0=h0: v.tensor_copy(
                NW_b[:, h0 * 2048:(h0 + 1) * 2048], PS_B[:, 0:2048]),
                waits=[tv], inc=vv)
        S.join()

        zpad = carve(B1_OFF, 0, "zpad", [128, 16], BF16)
        S.op("vector", lambda v: v.memset(zpad[:, :], 0.0), inc=vv)
        for ot in range(33):
            io_store(d["d_raw"][ot][:, 0:16], zpad[:, :], waits=[(vv, vv.n)])
        marks = [tv.n, tv.n]
        m3m = [vv.n, vv.n]
        for ot in range(33):
            par = ot % 2
            src = w["wxs_d"][ot] if ot < 32 else w["wbc_d"]
            g_load(WBUF[par][:, :], src, ctr=gW[par], waits=[(tv, marks[par])])
            ps = PS_A if par == 0 else PS_B
            for fc in range(16):
                for nb in range(4):
                    S.op("tensor", lambda tn, fc=fc, nb=nb, par=par, ps=ps: tn.matmul(
                        ps[:, nb * 512:(nb + 1) * 512],
                        WBUF[par][:, fc * 128:(fc + 1) * 128],
                        XL[fc][:, nb * 512:(nb + 1) * 512],
                        start=(fc == 0), stop=(fc == 15)),
                        waits=[(gW[par], gW[par].n), (vv, m3m[par])],
                        inc=(tv if (fc == 15 and nb == 3) else None))
            marks[par] = tv.n
            for half in range(2):
                S.op("vector", lambda v, ps=ps, half=half: v.tensor_tensor(
                    OBUF[half][:, :], ps[:, half * 1024:(half + 1) * 1024],
                    RHO3_b[:, half * 1024:(half + 1) * 1024], op=ALU.mult),
                    waits=[tv, (ioT[half], ioT[half].n)],
                    inc=(vv if half == 1 else None))
                io_store(d["d_raw"][ot][:, 16 + half * 1024:16 + (half + 1) * 1024],
                         OBUF[half][:, :], waits=[(vv, vv.n)], parity=half)
                if half == 1:
                    m3m[par] = vv.n
                if ot == 0:
                    tap(f"m3ob{half}", OBUF[half][:, :])
                    if DEBUG:
                        S.join()
        S.join()

        for c in range(16):
            io_store(d["d_x"][c], X[c][:, :])
        XSP = ioS.n
        S.join()

        XT_c = carve(X_OFF, 0, "XT_c", [128, 4096], BF16)
        y_c = carve(X_OFF, 8192, "y_c", [128, 4096], BF16)
        gate_c = carve(X_OFF, 16384, "gate_c", [128, 4096], BF16)
        D_q = carve(X_OFF, 24576, "D_q", [128, 2048], F32)
        MtL = carve(X_OFF, 32768, "MtL", [128, 2048], BF16)
        Lt = carve(X_OFF, 36864, "Lt", [128, 2048], BF16)
        prev = carve(X_OFF, 40960, "prev", [64, 4096], BF16)
        Bfm = carve(X_OFF, 49152, "Bfm", [64, 2048], BF16)
        Cfm = carve(X_OFF, 53248, "Cfm", [64, 2048], BF16)
        xd_q = carve(X_OFF, 57344, "xd_q", [128, 1024], BF16)
        t1q = carve(X_OFF, 59392, "t1q", [64, 1024], BF16)
        cacc = carve(X_OFF, 61440, "cacc", [128, 512], BF16)
        cacc2 = carve(X_OFF, 64480, "cacc2", [128, 512], BF16)
        cacc3 = carve(B1_OFF, 59520, "cacc3", [128, 512], BF16)
        cacc4 = carve(B1_OFF, 60544, "cacc4", [128, 512], BF16)
        eps_c = carve(X_OFF, 65504, "eps_c", [128, 1], F32)
        Btm = carve(X_OFF, 62464, "Btm", [128, 64], BF16)
        sdT = carve(X_OFF, 62592, "sdT", [128, 64], F32)
        dsT = carve(X_OFF, 62848, "dsT", [128, 64], F32)
        d1t = carve(X_OFF, 63104, "d1t", [128, 64], F32)
        DCOL = carve(X_OFF, 63360, "DCOL", [128, 64], F32)
        Mtm = carve(X_OFF, 63616, "Mtm", [128, 128], BF16)
        alast_r = carve(X_OFF, 63904, "alast_r", [1, 64], F32)
        cdrow = carve(X_OFF, 64160, "cdrow", [1, 64], F32)
        drow_t = carve(X_OFF, 64416, "drow_t", [1, 64], F32)
        raw_w = [carve(B1_OFF, 1056 * i, f"raw{i}", [128, 516], BF16)
                 for i in range(33)]
        yTall = carve(B2_OFF, 0, "yTall", [128, 32768], BF16)
        yT3 = yTall[:, :].rearrange("p (i n) -> p i n", n=1024)

        self.AFM = tv.n
        io_load(drow_t[:, :], w["drow_d"][:, :], parity=0, waits=[(ioS, XSP)])
        S.op("tensor", lambda tn: tn.matmul(PS_A[:, 0:64], onesr[:, :],
                                            drow_t[0:1, :], start=True, stop=True),
             waits=[(io[0], io[0].n), (tv, tv.n)], inc=tv)
        S.op("vector", lambda v: v.tensor_copy(DCOL[:, :], PS_A[:, 0:64]),
             waits=[tv])
        S.op("vector", lambda v: v.memset(prev[:, :], 0.0))
        S.op("vector", lambda v: v.memset(eps_c[:, :], EPS_RMS), inc=vv)
        S.join()

        self.pending_epi = None
        self.SQM = sv.n
        self.YTRM = tv.n
        self.YTM = sv.n
        self.PSBM = vv.n
        self.DMARK = vv.n
        self.TOFFM = vv.n
        self.EXPM = sv.n
        self.MIDM = vv.n
        self.STM = tv.n
        self.pend_AFL = None
        self.XTCM = sv.n
        self.silu_m = [0, 0, 0, 0]
        self.CONVS = sv.n
        self.CONVV = vv.n

        def emit_chunk_epi():
            if self.pending_epi is None:
                return
            c, GATEC = self.pending_epi
            self.pending_epi = None
            sq_scr = D_q.bitcast(BF16)
            S.op("scalar", lambda s: s.activation(
                out=sq_scr[:, 0:4096], in_=y_c[:, :], func=AF.Square,
                accum_out=d1t[:, 0:1]),
                waits=[(vv, vv.n), (pv, pv.n)], inc=sv)
            self.SQM = sv.n
            S.op("scalar", lambda s: s.activation(
                out=d1t[:, 2:3], in_=d1t[:, 0:1], func=AF.Sqrt,
                scale=1.0 / 4096.0, bias=eps_c[:, 0:1]), inc=sv)
            S.op("vector", lambda v: v.reciprocal(d1t[:, 3:4], d1t[:, 2:3]),
                 waits=[(sv, sv.n)])
            S.op("vector", lambda v: v.scalar_tensor_tensor(
                y_c[:, :], y_c[:, :], d1t[:, 3:4], NW_b[:, :],
                op0=ALU.mult, op1=ALU.mult), inc=vv)
            if c == 0:
                tap("yc0", y_c[:, 0:2048])
                tap("ycB", y_c[:, 2048:4096])
                if DEBUG:
                    S.join()
            for ic in range(32):
                S.op("tensor", lambda tn, ic=ic: tn.transpose(
                    PS_Bbf[:, ic * 128:(ic + 1) * 128],
                    y_c[:, ic * 128:(ic + 1) * 128], identb[:, :]),
                    waits=([(vv, vv.n), (sv, self.XTCM)] if ic == 0 else []),
                    inc=(tv if ic == 31 else None))
            self.YTRM = tv.n
            blk = c % 8
            S.op("scalar", lambda s, blk=blk: s.activation(
                out=yT3[:, :, blk * 128:(blk + 1) * 128],
                in_=PS_Bbf[:, 0:4096].rearrange("p (i n) -> p i n", n=128),
                func=AF.Copy), waits=[tv], inc=sv)
            self.YTM = sv.n

        def aflat_load(c, q):
            S.op("scalar", lambda g, c=c, q=q: g.dma_start(
                out=aflat[:, :],
                in_=d["d_flat"][c].rearrange("a b -> (a b)")
                .unsqueeze(0)[0:1, q * 2048:(q + 1) * 2048]),
                waits=[(tv, self.AFM)], inc=gA, inc_k=16, no_own=True)
            self.pend_AFL = gA.n

        def emit_A(c, q, extra=()):
            if self.pend_AFL is None:
                aflat_load(c, q)
            AFL = self.pend_AFL
            self.pend_AFL = None
            for nb in range(4):
                S.op("tensor", lambda tn, nb=nb: tn.matmul(
                    PS_A[:, nb * 512:(nb + 1) * 512], onesr[:, :],
                    aflat[0:1, nb * 512:(nb + 1) * 512],
                    start=True, stop=True),
                    waits=(([(gA, AFL), (vv, self.DMARK)]
                            + list(extra)) if nb == 0 else []),
                    inc=(tv if nb == 3 else None))
            self.AFM = tv.n
            S.op("vector", lambda v, c=c, q=q: v.tensor_tensor(
                D_q[:, :].rearrange("p (h l) -> p h l", h=16),
                acumT[:, c * 64 + q * 16:c * 64 + (q + 1) * 16]
                .unsqueeze(2).broadcast_to([128, 16, 128]),
                PS_A[:, :].rearrange("p (h l) -> p h l", h=16),
                op=ALU.subtract),
                waits=[tv, (sv, sv.n)], inc=vv)
            self.DMARK = vv.n
            S.op("scalar", lambda s: s.activation(
                out=D_q[:, :], in_=D_q[:, :], func=AF.Relu),
                waits=[(vv, vv.n)])
            S.op("scalar", lambda s: s.activation(
                out=Lt[:, :], in_=D_q[:, :], func=AF.Exp, scale=-1.0),
                inc=sv)
            self.EXPM = sv.n

        def emit_Mid(c, q):
            S.op("vector", lambda v: v.tensor_tensor(
                MtL[:, :].rearrange("p (h l) -> p h l", h=16),
                Lt[:, :].rearrange("p (h l) -> p h l", h=16),
                Mtm[:, :].unsqueeze(1).broadcast_to([128, 16, 128]),
                op=ALU.mult),
                waits=[(sv, max(self.EXPM, self.XTCM)), (tv, self.STM)],
                inc=vv)
            S.op("vector", lambda v, c=c, q=q: v.tensor_tensor(
                xd_q[:, :].rearrange("p (h e) -> p h e", h=16),
                XT_c[:, q * 1024:(q + 1) * 1024]
                .rearrange("p (h e) -> p h e", h=16),
                dtT[:, c * 64 + q * 16:c * 64 + (q + 1) * 16]
                .unsqueeze(2).broadcast_to([128, 16, 64]),
                op=ALU.mult), inc=vv)
            self.MIDM = vv.n

        def emit_iter(c, q, nxt):
            qc = slice(q * 16 * 64, (q + 1) * 16 * 64)
            if nxt is not None:
                emit_A(*nxt)
            for j in range(16):
                S.op("tensor", lambda tn, j=j: tn.matmul(
                    PS_B[:, j * 64:(j + 1) * 64],
                    MtL[:, j * 128:(j + 1) * 128],
                    xd_q[:, j * 64:(j + 1) * 64],
                    start=True, stop=True),
                    waits=([(vv, max(self.PSBM, self.MIDM)), (sv, self.YTM)]
                           if j == 0 else []),
                    inc=(tv if j == 15 else None))
            for j in range(16):
                S.op("tensor", lambda tn, j=j, c=c, q=q: tn.matmul(
                    PS_B[:, 1024 + j * 64:1024 + (j + 1) * 64],
                    Cfm[:, c * 128:(c + 1) * 128],
                    prev[:, (q * 16 + j) * 64:(q * 16 + j + 1) * 64],
                    start=True, stop=True),
                    inc=(tv if j == 15 else None))
            S.op("vector", lambda v, q=q: v.tensor_tensor(
                xd_q[:, :].rearrange("p (h e) -> p h e", h=16),
                xd_q[:, :].rearrange("p (h e) -> p h e", h=16),
                dsT[:, q * 16:(q + 1) * 16]
                .unsqueeze(2).broadcast_to([128, 16, 64]),
                op=ALU.mult), waits=[tv], inc=vv)
            S.op("vector", lambda v, q=q: v.tensor_tensor(
                MtL[:, 1024:2048].rearrange("p (h e) -> p h e", h=16),
                PS_B[:, 1024:2048].rearrange("p (h e) -> p h e", h=16),
                sdT[:, q * 16:(q + 1) * 16]
                .unsqueeze(2).broadcast_to([128, 16, 64]),
                op=ALU.mult), inc=vv)
            self.TOFFM = vv.n
            S.op("vector", lambda v: v.tensor_tensor(
                MtL[:, 0:1024], PS_B[:, 0:1024], MtL[:, 1024:2048],
                op=ALU.add), inc=vv)
            T2M = vv.n
            for j in range(16):
                S.op("tensor", lambda tn, j=j: tn.matmul(
                    PS_B[0:64, j * 64:(j + 1) * 64], Btm[:, :],
                    xd_q[:, j * 64:(j + 1) * 64],
                    start=True, stop=True),
                    waits=([(vv, T2M)] if j == 0 else []),
                    inc=(tv if j == 15 else None))
            S.op("tensor", lambda tn: tn.matmul(
                PS_B[0:64, 1024:1088], ones64[:, :], cdrow[0:1, :],
                start=True, stop=True), inc=tv)
            self.STM = tv.n
            S.op("vector", lambda v, q=q: v.tensor_tensor(
                MtL[:, 1024:2048].rearrange("p (h e) -> p h e", h=16),
                XT_c[:, q * 1024:(q + 1) * 1024]
                .rearrange("p (h e) -> p h e", h=16),
                DCOL[:, q * 16:(q + 1) * 16]
                .unsqueeze(2).broadcast_to([128, 16, 64]),
                op=ALU.mult), inc=vv)
            S.op("vector", lambda v, q=q: v.tensor_tensor(
                y_c[:, q * 1024:(q + 1) * 1024], MtL[:, 0:1024],
                MtL[:, 1024:2048], op=ALU.add),
                waits=[(tv, self.YTRM)], inc=vv)
            S.op("gpsimd", lambda g, q=q: g.tensor_tensor(
                y_c[:, q * 1024:(q + 1) * 1024],
                y_c[:, q * 1024:(q + 1) * 1024],
                gate_c[:, q * 1024:(q + 1) * 1024], op=ALU.mult),
                waits=[(gG, self.GATEC), (vv, vv.n)], inc=pv)
            if nxt is not None and nxt[0] == c:
                emit_Mid(*nxt)
            S.op("vector", lambda v, q=q: v.tensor_tensor(
                t1q[:, :].rearrange("p (h e) -> p h e", h=16),
                prev[:, qc].rearrange("p (h e) -> p h e", h=16),
                PS_B[0:64, 1024 + q * 16:1024 + (q + 1) * 16]
                .unsqueeze(2).broadcast_to([64, 16, 64]),
                op=ALU.mult), waits=[tv], inc=vv)
            S.op("vector", lambda v: v.tensor_tensor(
                prev[:, qc], t1q[:, :], PS_B[0:64, 0:1024],
                op=ALU.add), inc=vv)
            self.PSBM = vv.n

        wopb = [carve(X_OFF, 8192, "wopb0", [128, 4096], BF16),
                carve(X_OFF, 16384, "wopb1", [128, 4096], BF16)]
        self.pend_IOW = None
        for half in range(2):
            for wi in range(2):
                win = half * 2 + wi
                if self.pend_IOW is None:
                    for ot in range(33):
                        io_load(raw_w[ot][:, 0:515],
                                d["d_raw"][ot][:, 13 + 512 * win:13 + 512 * win + 515],
                                parity=ot % 2,
                                waits=[(tv, tv.n), (vv, vv.n), (sv, sv.n)])
                    self.pend_IOW = (io[0].n, io[1].n)
                IOW = self.pend_IOW
                self.pend_IOW = None
                if win == 0:
                    tap("rawpre", raw_w[0][:, 0:516])
                    if DEBUG:
                        S.join()
                cacc_b = [cacc, cacc2, cacc3, cacc4]
                for ot in range(33):
                    ca = cacc_b[ot % 4]
                    S.op("vector", lambda v, ot=ot, ca=ca: v.tensor_scalar_mul(
                        ca[:, :], raw_w[ot][:, 3:515],
                        cw_t[:, ot * 4 + 3:ot * 4 + 4]),
                        waits=[(io[0], IOW[0]), (io[1], IOW[1]),
                               (sv, self.silu_m[ot % 4])])
                    for k in range(3):
                        S.op("vector", lambda v, ot=ot, k=k, ca=ca:
                             v.scalar_tensor_tensor(
                                 ca[:, :], raw_w[ot][:, k:512 + k],
                                 cw_t[:, ot * 4 + k:ot * 4 + k + 1], ca[:, :],
                                 op0=ALU.mult, op1=ALU.add))
                    S.op("scalar", lambda s, ot=ot, ca=ca: s.activation(
                        out=raw_w[ot][:, 3:515], in_=ca[:, :], func=AF.Silu,
                        bias=cb_t[:, ot:ot + 1]), waits=[(vv, vv.n)], inc=sv)
                    self.silu_m[ot % 4] = sv.n
                S.op("vector", lambda v, win=win: v.tensor_copy(
                    Bfm[:, 512 * win:512 * win + 512], raw_w[32][0:64, 3:515]),
                    waits=[(sv, sv.n)])
                S.op("vector", lambda v, win=win: v.tensor_copy(
                    Cfm[:, 512 * win:512 * win + 512], raw_w[32][64:128, 3:515]),
                    inc=vv)
                self.CONVS = sv.n
                self.CONVV = vv.n
                if win == 0:
                    tap("rawc0", raw_w[0][:, 0:516])
                    tap("BCfm", Bfm[0:64, :])
                    if DEBUG:
                        S.join()

                for cc in range(4):
                    c = win * 4 + cc
                    if c == half * 8:
                        emit_A(c, 0, extra=[(sv, sv.n)])
                    for ot in range(32):
                        S.op("tensor", lambda tn, ot=ot, cc=cc: tn.transpose(
                            PS_Bbf[:, ot * 128:(ot + 1) * 128],
                            raw_w[ot][:, 3 + cc * 128:3 + cc * 128 + 128],
                            identb[:, :]),
                            waits=([(sv, max(self.CONVS, self.YTM)),
                                    (vv, self.PSBM)]
                                   if ot == 0 else []),
                            inc=(tv if ot == 31 else None))
                    S.op("scalar", lambda s: s.activation(
                        out=XT_c[:, :], in_=PS_Bbf[:, 0:4096], func=AF.Copy),
                        waits=[tv], inc=sv)
                    self.XTCM = sv.n
                    emit_chunk_epi()
                    S.op("tensor", lambda tn, c=c: tn.transpose(
                        PS_Abf[:, 0:64], Bfm[:, c * 128:(c + 1) * 128],
                        identb[0:64, 0:64]),
                        waits=[(vv, max(self.DMARK, self.CONVV))], inc=tv)
                    S.op("vector", lambda v: v.tensor_copy(Btm[:, :],
                                                           PS_Abf[:, 0:64]),
                         waits=[tv], inc=vv)
                    S.op("tensor", lambda tn, c=c: tn.matmul(
                        PS_A[:, 128:256], Bfm[:, c * 128:(c + 1) * 128],
                        Cfm[:, c * 128:(c + 1) * 128], start=True, stop=True),
                        inc=tv)
                    S.op("vector", lambda v: v.tensor_tensor(
                        Mtm[:, :], PS_A[:, 128:256], maskb[:, :], op=ALU.mult),
                        waits=[tv], inc=vv)
                    S.op("tensor", lambda tn, c=c: tn.transpose(
                        PS_A[0:1, 64:128], alast_c[:, c:c + 1], identf[0:64, 0:64]),
                        inc=tv)
                    S.op("vector", lambda v: v.tensor_copy(alast_r[:, :],
                                                           PS_A[0:1, 64:128]),
                         waits=[tv], inc=vv)
                    S.op("scalar", lambda s: s.activation(
                        out=cdrow[:, :], in_=alast_r[:, :], func=AF.Exp),
                        waits=[(vv, vv.n)], inc=sv)
                    S.op("tensor", lambda tn: tn.matmul(
                        PS_A[:, 0:64], onesr[:, :], alast_r[0:1, :],
                        start=True, stop=True), waits=[(vv, vv.n)], inc=tv)
                    S.op("vector", lambda v, c=c: v.tensor_tensor(
                        d1t[:, :], PS_A[:, 0:64], acumT[:, c * 64:(c + 1) * 64],
                        op=ALU.subtract), waits=[tv], inc=vv)
                    S.op("scalar", lambda s: s.activation(
                        out=dsT[:, :], in_=d1t[:, :], func=AF.Exp),
                        waits=[(vv, vv.n)], inc=sv)
                    S.op("scalar", lambda s, c=c: s.activation(
                        out=sdT[:, :], in_=acumT[:, c * 64:(c + 1) * 64],
                        func=AF.Exp), waits=[(vv, self.TOFFM)], inc=sv)
                    S.op("sync", lambda g, c=c: g.dma_start(
                        out=gate_c[:, :].rearrange("p (q n) -> p q n", q=4),
                        in_=d["d_gate"][c].rearrange("q p n -> p q n")),
                        waits=[(vv, vv.n), (sv, sv.n), (pv, pv.n)],
                        inc=gG, inc_k=16)
                    GATEC = gG.n
                    self.GATEC = GATEC
                    if cc == 3 and win < 3:
                        nwin = win + 1
                        for ot in range(33):
                            io_load(raw_w[ot][:, 0:515],
                                    d["d_raw"][ot]
                                    [:, 13 + 512 * nwin:13 + 512 * nwin + 515],
                                    parity=ot % 2, waits=[(tv, tv.n)])
                        self.pend_IOW = (io[0].n, io[1].n)
                    self.DMARK = max(self.DMARK, vv.n)
                    emit_Mid(c, 0)
                    for q in range(4):
                        if q < 3:
                            nxt = (c, q + 1)
                        elif (c + 1) % 8 != 0:
                            nxt = (c + 1, 0)
                        else:
                            nxt = None
                        emit_iter(c, q, nxt)
                    self.pending_epi = (c, GATEC)
                    if cc == 3 and wi == 1:
                        if half == 0:
                            aflat_load(8, 0)
                        emit_chunk_epi()

            marks5 = [tv.n, tv.n]
            for ot in range(16):
                par = ot % 2
                g_load(wopb[par][:, :], w["wop_d"][ot], ctr=gW[par],
                       waits=[(tv, marks5[par])])
                ps = PS_A if par == 0 else PS_B
                for ic in range(32):
                    for nb in range(2):
                        S.op("tensor", lambda tn, ic=ic, nb=nb, par=par, ps=ps: tn.matmul(
                            ps[:, nb * 512:(nb + 1) * 512],
                            wopb[par][:, ic * 128:(ic + 1) * 128],
                            yTall[:, ic * 1024 + nb * 512:ic * 1024 + (nb + 1) * 512],
                            start=(ic == 0), stop=(ic == 31)),
                            waits=[(gW[par], gW[par].n), (sv, sv.n)],
                            inc=(tv if (ic == 31 and nb == 1) else None))
                marks5[par] = tv.n
                S.op("scalar", lambda s, ps=ps, par=par: s.activation(
                    out=OBUF[par][:, :], in_=ps[:, 0:1024], func=AF.Copy),
                    waits=[tv, (ioT[par], ioT[par].n)], inc=sv)
                io_store(d["d_om"][ot][:, half * 1024:(half + 1) * 1024],
                         OBUF[par][:, :], waits=[(sv, sv.n)], parity=par)
            S.join()

        r1prow = carve(B1_OFF, 12288, "r1prow", [1, 2048], F32)
        omt = [carve(B1_OFF, 0, "omt0", [128, 2048], BF16),
               carve(B1_OFF, 4096, "omt1", [128, 2048], BF16)]
        tres = carve(B1_OFF, 8192, "tres", [128, 2048], BF16)
        io_load(r1prow[:, :], d["d_rows"][rows["ROW_R1P"]:rows["ROW_R1P"] + 1, :],
                parity=0, waits=[(ioS, ioS.n), (ioT[0], ioT[0].n),
                                 (ioT[1], ioT[1].n)])
        for nb in range(4):
            S.op("tensor", lambda tn, nb=nb: tn.matmul(
                PS_A[:, nb * 512:(nb + 1) * 512], onesr[:, :],
                r1prow[0:1, nb * 512:(nb + 1) * 512], start=True, stop=True),
                waits=[(io[0], io[0].n), (tv, tv.n)],
                inc=(tv if nb == 3 else None))
        vmarks6 = [vv.n, vv.n]
        for ot in range(16):
            par = ot % 2
            io_load(X[ot][:, :], d["d_x"][ot], parity=par,
                    waits=[(tv, tv.n), (vv, vmarks6[par])])
            io_load(omt[par][:, :], d["d_om"][ot], parity=par,
                    waits=[(vv, vmarks6[par])])
            S.op("vector", lambda v, ot=ot, par=par: v.tensor_tensor(
                tres[:, :], X[ot][:, :], PS_A[:, :], op=ALU.mult),
                waits=[(io[par], io[par].n)])
            S.op("vector", lambda v, ot=ot, par=par: v.tensor_tensor(
                X[ot][:, :], tres[:, :], omt[par][:, :], op=ALU.add), inc=vv)
            vmarks6[par] = vv.n
        S.join()
        tap("xout_mamba", X[0][:, :])




def _stage_weights(inputs):
    w = {}
    wqkv = inputs["L0_attn_a_w_qkv"]
    b_qkv = inputs["L0_attn_a_b_qkv"]

    def lhsT_blk(W):
        O, F_ = W.shape
        return np.ascontiguousarray(
            W.reshape(O // 128, 128, F_ // 128, 128).transpose(0, 3, 2, 1)
            .reshape(O // 128, 128, F_))

    w["wq_blk"] = lhsT_blk(wqkv[0:2048])
    w["wk_blk"] = lhsT_blk(wqkv[2048:4096])
    wv = wqkv[4096:6144]
    w["wv_blk"] = np.ascontiguousarray(
        wv.reshape(2, 1024, 16, 128).transpose(0, 2, 3, 1))
    w["bq_cols"] = np.ascontiguousarray(b_qkv[0:2048].reshape(16, 128).T)
    w["bk_cols"] = np.ascontiguousarray(b_qkv[2048:4096].reshape(16, 128).T)
    w["bv_cols"] = np.ascontiguousarray(b_qkv[4096:6144].reshape(16, 128).T)
    w["wout_blk"] = lhsT_blk(inputs["L0_attn_a_w_out"])
    w["bout_cols"] = np.ascontiguousarray(inputs["L0_attn_a_b_out"].reshape(16, 128).T)

    rt = inputs["L0_ffn_router_w"].T
    w["router_wT"] = np.ascontiguousarray(
        rt.reshape(16, 128, 8).transpose(1, 0, 2).reshape(128, 128))
    w["w1_blk"] = np.ascontiguousarray(
        inputs["L0_ffn_w1"].reshape(8, 16, 128, 4, 128).transpose(0, 3, 2, 1, 4)
        .reshape(8, 4, 128, 2048))
    w["w3_blk"] = np.ascontiguousarray(
        inputs["L0_ffn_w3"].reshape(8, 16, 128, 4, 128).transpose(0, 3, 2, 1, 4)
        .reshape(8, 4, 128, 2048))
    w["w2_blk"] = np.ascontiguousarray(
        inputs["L0_ffn_w2"].reshape(8, 4, 128, 16, 128).transpose(0, 3, 2, 1, 4)
        .reshape(8, 16, 128, 512))

    wip = inputs["L0_attn_b_in_proj_w"]
    w["wg_blk"] = np.ascontiguousarray(
        wip[0:4096].reshape(2, 2048, 16, 128).transpose(0, 2, 3, 1))
    w["wxs_blk"] = lhsT_blk(wip[4096:8192])
    w["wbc_blk"] = np.ascontiguousarray(
        wip[8192:8320].reshape(128, 16, 128).transpose(2, 1, 0).reshape(128, 2048))
    w["wdt_blk"] = np.ascontiguousarray(
        wip[8320:8384].reshape(64, 16, 128).transpose(1, 2, 0))
    w["convw_cols"] = np.ascontiguousarray(
        inputs["L0_attn_b_conv_w"].reshape(33, 128, 4).transpose(1, 0, 2)
        .reshape(128, 132))
    w["convb_cols"] = np.ascontiguousarray(
        inputs["L0_attn_b_conv_b"].reshape(33, 128).T)
    w["dtbias_col"] = np.ascontiguousarray(inputs["L0_attn_b_dt_bias"].reshape(64, 1))
    w["alog_col"] = np.ascontiguousarray(inputs["L0_attn_b_A_log"].reshape(64, 1))
    w["d_row"] = np.ascontiguousarray(inputs["L0_attn_b_D"][None, :])
    w["nw_row"] = np.ascontiguousarray(inputs["L0_attn_b_norm_weight"][None, :])
    w["lnw_cols"] = np.ascontiguousarray(
        inputs["L0_attn_b_ln_weight"].reshape(16, 128).T)
    w["wop_blk"] = np.ascontiguousarray(
        inputs["L0_attn_b_out_proj_w"].reshape(16, 128, 32, 128)
        .transpose(0, 3, 2, 1).reshape(16, 128, 4096))
    return {k: np.ascontiguousarray(v, dtype=np.float32) for k, v in w.items()}


def kernel(**inputs):
    hs = np.asarray(inputs["L0_attn_a_hidden_states"], dtype=np.float32)
    wmap = _stage_weights({k: np.asarray(v, dtype=np.float32)
                           for k, v in inputs.items()
                           if k != "L0_attn_a_hidden_states"})
    nc = build_nc()
    in_maps = []
    for b in range(NCORES):
        m = dict(wmap)
        m["x_fm"] = np.ascontiguousarray(hs[b].T).reshape(NC, 128, T)
        in_maps.append(m)
    res = run_bass_kernel_spmd(nc, in_maps, core_ids=list(range(NCORES)))
    kernel._last = res
    outs = []
    for b in range(NCORES):
        o = res.results[b]["out_fm"].reshape(H, T)
        outs.append(np.ascontiguousarray(o.T))
    return np.stack(outs, axis=0)


if __name__ == "__main__":
    print("kernel module ok")
